# version 05
# baseline (speedup 1.0000x reference)
import numpy as np
import concourse.bass as bass
import concourse.mybir as mybir
from concourse.bass_utils import run_bass_kernel_spmd
from concourse.alu_op_type import AluOpType
from contextlib import ExitStack

F32 = mybir.dt.float32
BF16 = mybir.dt.bfloat16
AF = mybir.ActivationFunctionType
OP = AluOpType

EPS = 1e-5
NTOK = 1024
NLOC = 1027
NT = 8
NC_D = 32
T = 128
NH = 64
S = 128
D_MODEL = 4096
PROJ = 8512
CONV = 4352
NCC = 34
XNW = 1028

STAGE = 7
DEBUG = False


class Sem:
    def __init__(self, nc, es, name):
        self.h = es.enter_context(nc.semaphore(name))
        self.n = 0

    def now(self):
        return (self, self.n)


def wait_evt(eng, evt):
    if evt is not None:
        sem, n = evt
        if n > 0:
            eng.wait_ge(sem.h, n)


def wait_evts(eng, *evts):
    best = {}
    for e in evts:
        if e is None:
            continue
        sem, n = e
        if n > 0 and (sem not in best or best[sem] < n):
            best[sem] = n
    for sem, n in best.items():
        eng.wait_ge(sem.h, n)


class MK:
    def __init__(self, stage=7, debug=False):
        self.stage = stage
        self.debug = debug
        nc = bass.Bass("TRN2", target_bir_lowering=False)
        self.nc = nc
        self.es = ExitStack()
        self.dbg_items = []
        self._mk_dram()
        self._mk_sbuf()
        self._mk_psum()
        self._mk_sems()
        self.block = self.es.enter_context(nc.Block())
        self.unit_done = {}

    def _mk_dram(self):
        nc = self.nc
        D = lambda n, s, d, k=None: (nc.dram_tensor(n, s, d, kind=k).ap() if k
                                     else nc.dram_tensor(n, s, d).ap())
        self.x_in = D("x_in", [NLOC, D_MODEL], F32, "ExternalInput")
        self.Wp = D("Wp", [D_MODEL, PROJ], F32, "ExternalInput")
        self.conv_w = D("conv_w", [CONV, 4], F32, "ExternalInput")
        self.A_log = D("A_log", [NH], F32, "ExternalInput")
        self.dt_b = D("dt_b", [NH], F32, "ExternalInput")
        self.D_in = D("D_in", [NH], F32, "ExternalInput")
        self.ln_w = D("ln_w", [D_MODEL], F32, "ExternalInput")
        self.norm_w = D("norm_w", [D_MODEL], F32, "ExternalInput")
        self.Wout = D("Wout", [D_MODEL, D_MODEL], F32, "ExternalInput")
        self.mup = D("mup", [PROJ], F32, "ExternalInput")
        self.Wg = D("Wg", [D_MODEL, 8], F32, "ExternalInput")
        self.bg = D("bg", [8], F32, "ExternalInput")
        self.W_e = D("W_e", [8, D_MODEL, D_MODEL], F32, "ExternalInput")
        self.b_e = D("b_e", [8, D_MODEL], F32, "ExternalInput")
        self.selv = D("selv", [NH, 4], F32, "ExternalInput")
        self.mulv = D("mulv", [NH, 4], F32, "ExternalInput")
        self.out = D("out", [NTOK, D_MODEL], F32, "ExternalOutput")
        self.z_scr = D("z_scr", [NTOK, D_MODEL], BF16)
        self.xs_scr = D("xs_scr", [NTOK, D_MODEL], BF16)
        self.sloc_scr = D("sloc_scr", [7, S, D_MODEL], F32)
        self.gnt_scr = D("gnt_scr", [NT, 8, 128, 512], BF16)
        self.ffn_scr = D("ffn_scr", [NTOK, D_MODEL], F32)
        self.cum_scr = D("cum_scr", [NH, NTOK], F32)
        self.woutbf = D("woutbf", [D_MODEL, D_MODEL], BF16)
        self.ag_in = D("ag_in", [S, D_MODEL], F32)
        self.ag_out = D("ag_out", [4 * S, D_MODEL], F32)
        self.agp_in = D("agp_in", [NH, 1], F32)
        self.agp_out = D("agp_out", [4 * NH, 1], F32)

    def _mk_sbuf(self):
        nc, es = self.nc, self.es
        SB = lambda n, s, d: es.enter_context(nc.sbuf_tensor(n, s, d))
        self.xnT = SB("xnT", [128, NC_D * XNW], BF16)
        self.Wbuf = SB("Wbuf", [128, 3 * 8192], BF16)
        self.S32a = SB("S32a", [128, 4096], F32)
        self.S32b = SB("S32b", [128, 4096], F32)
        self.Sbf_a = SB("Sbf_a", [128, 4096], BF16)
        self.Sbf_b = SB("Sbf_b", [128, 4096], BF16)
        self.scr32 = SB("scr32", [128, 1024], F32)
        self.B_fm = SB("B_fm", [128, XNW], BF16)
        self.C_fm = SB("C_fm", [128, XNW], BF16)
        self.B_tok = SB("B_tok", [128, 1024], BF16)
        self.GT_all = SB("GT_all", [128, 1024], BF16)
        self.dt_sb = SB("dt_sb", [NH, 1024], F32)
        self.cum = SB("cum", [NH, 1024], F32)
        self.cs2 = SB("cs2", [NH, 1024], F32)
        self.zeros64 = SB("zeros64", [NH, T], F32)
        self.transT = SB("transT", [128, 128], F32)
        self.ecumT = SB("ecumT", [128, NH], F32)
        self.wq = SB("wq", [NH, T], F32)
        self.wqT = SB("wqT", [128, NH], F32)
        self.aTbc = SB("aTbc", [128, NH], F32)
        self.Qbc = SB("Qbc", [128, 8 * NH], F32)
        self.Pcol = SB("Pcol", [NH, 2], F32)
        self.hsbuf = SB("hsbuf", [128, 2 * 512], BF16)
        self.wbc = SB("wbc", [128, 4 * NH], F32)
        self.gsm = SB("gsm", [NH, 16], F32)
        self.rows = SB("rows", [1, 1024], F32)
        self.lam = SB("lam", [128, 1024], BF16)
        self.mup_fm = SB("mup_fm", [128, NCC + 2], F32)
        self.mup_dt = SB("mup_dt", [NH, 1], F32)
        self.r2bc = SB("r2bc", [128, XNW], F32)
        self.r2col = SB("r2col", [128, 9], F32)
        self.stat = SB("stat", [128, 16], F32)
        self.lnw_c = SB("lnw_c", [128, NC_D], F32)
        self.nw_c = SB("nw_c", [128, NC_D], F32)
        self.Acol = SB("Acol", [NH, 1], F32)
        self.dtbcol = SB("dtbcol", [NH, 1], F32)
        self.Dbc = SB("Dbc", [128, NH], F32)
        self.convw = SB("convw", [128, NCC * 4], F32)
        self.ident_f = SB("ident_f", [128, 128], F32)
        self.ident_b = SB("ident_b", [128, 128], BF16)
        self.pen = SB("pen", [128, 128], F32)
        self.ones_f = SB("ones_f", [1, 128], F32)
        self.ones_b = SB("ones_b", [1, 128], BF16)
        self.masks = SB("masks", [128, NT * 8], F32)
        self.mx = SB("mx", [128, 2], F32)
        self.Wg_sb = SB("Wg_sb", [128, NC_D * 8], BF16)
        self.row1 = SB("row1", [1, 128], F32)
        self.brow = SB("brow", [1, 520], BF16)

    def _mk_psum(self):
        nc, es = self.nc, self.es
        P = lambda n, s, d: es.enter_context(nc.psum_tensor(n, s, d))
        self.pb = [P(f"pb{i}", [128, 512], F32) for i in range(3)]
        self.paux = P("paux", [128, 512], F32)
        self.pscan = P("pscan", [128, 512], F32)
        self.pY = P("pY", [128, 512], F32)
        self.ptrA = P("ptrA", [128, 1024], BF16)
        self.ptrB = P("ptrB", [128, 1024], BF16)
        self.ptr = [self.ptrA, self.ptrB]

    def _mk_sems(self):
        nc, es = self.nc, self.es
        mk = lambda n: Sem(nc, es, n)
        self.s_pre = mk("s_pre")
        self.s_ld = mk("s_ld")
        self.s_w = mk("s_w")
        self.s_wsc = mk("s_wsc")
        self.s_st = mk("s_st")
        self.s_gst = mk("s_gst")
        self.s_pe = mk("s_pe")
        self.s_dve = mk("s_dve")
        self.s_act = mk("s_act")
        self.s_pool = mk("s_pool")
        self.s_cc = mk("s_cc")
        self.s_rows = mk("s_rows")
        self.s_fin = mk("s_fin")

    def xnT_sl(self, kc, lo, n):
        return self.xnT[:, kc * XNW + lo: kc * XNW + lo + n]

    def build(self):
        self.p0_preamble()
        self.p1_layernorm()
        if self.stage >= 2:
            self.p2_inproj()
        if self.stage >= 3:
            self.p3_dtchain()
        if self.stage >= 4:
            self.p4_passA()
        if self.stage >= 5:
            self.p5_passB()
        if self.stage >= 6:
            self.p6_experts()
        if self.stage >= 7:
            self.p7_outproj()
        self.finalize_program()
        self.es.close()
        return self.nc

    def p0_preamble(self):
        mk = self
        blk = self.block

        @blk.gpsimd
        def _(g):
            for (dst, src) in [
                (mk.scr32[0:NC_D, 0:128], mk.ln_w.rearrange("(c p) -> c p", p=128)),
                (mk.scr32[0:NC_D, 128:256], mk.norm_w.rearrange("(c p) -> c p", p=128)),
                (mk.Acol[:, :], mk.A_log.rearrange("(h o) -> h o", o=1)),
                (mk.dtbcol[:, :], mk.dt_b.rearrange("(h o) -> h o", o=1)),
                (mk.convw.ap().rearrange("p (c k) -> p c k", k=4),
                 mk.conv_w.rearrange("(c p) k -> p c k", p=128)),
                (mk.scr32[0:NCC, 256:384], mk.mup[4096:8448].rearrange("(c p) -> c p", p=128)),
                (mk.mup_dt[:, :], mk.mup[8448:8512].rearrange("(h o) -> h o", o=1)),
                (mk.Wg_sb.ap().rearrange("p (c e) -> p c e", e=8),
                 mk.Wg.rearrange("(c p) e -> p c e", p=128)),
                (mk.row1[0:1, 0:NH], mk.D_in.rearrange("(o h) -> o h", o=1)),
                (mk.brow[0:1, 512:520], mk.bg.rearrange("(o e) -> o e", o=1)),
                (mk.gsm[:, 0:4], mk.selv[:, :]),
                (mk.gsm[:, 4:8], mk.mulv[:, :]),
            ]:
                g.dma_start(out=dst, in_=src).then_inc(mk.s_pre.h, 16)
                mk.s_pre.n += 16
            g.memset(mk.zeros64[:, :], 0.0)
            g.memset(mk.ones_f[:, :], 1.0)
            g.memset(mk.ones_b[:, :], 1.0)
            g.memset(mk.Qbc[:, 0:NH], 1.0)
            g.memset(mk.Pcol[:, 0:1], 1.0)
            g.memset(mk.mup_fm[:, NCC:NCC + 2], 0.0)
            g.iota(mk.ident_f[:, :], pattern=[[1, 128]], base=0, channel_multiplier=-1,
                   allow_small_or_imprecise_dtypes=True)
            g.iota(mk.pen[:, :], pattern=[[1, 128]], base=0, channel_multiplier=-1,
                   allow_small_or_imprecise_dtypes=True)
            g.engine_nop().then_inc(mk.s_pre.h, 1)
            mk.s_pre.n += 1

        @blk.vector
        def _(v):
            v.wait_ge(mk.s_pre.h, mk.s_pre.n)
            v.tensor_scalar(out=mk.ident_f[:, :], in0=mk.ident_f[:, :], scalar1=0.0,
                            scalar2=None, op0=OP.is_equal, op1=OP.bypass)
            v.tensor_scalar(out=mk.pen[:, :], in0=mk.pen[:, :], scalar1=0.0, scalar2=-1e9,
                            op0=OP.is_gt, op1=OP.mult)
            v.drain()
            v.tensor_copy(mk.ident_b[:, :], mk.ident_f[:, :]).then_inc(mk.s_dve.h, 1)
        mk.s_dve.n += 1
        ident_ready = mk.s_dve.now()

        @blk.scalar
        def _(sc):
            sc.wait_ge(mk.s_pre.h, mk.s_pre.n)
            sc.activation(out=mk.Acol[:, :], in_=mk.Acol[:, :], func=AF.Exp).then_inc(mk.s_act.h, 1)
        mk.s_act.n += 1

        @blk.vector
        def _(v):
            v.wait_ge(mk.s_act.h, mk.s_act.n)
            v.tensor_scalar(out=mk.Acol[:, :], in0=mk.Acol[:, :], scalar1=-1.0,
                            scalar2=None, op0=OP.mult, op1=OP.bypass).then_inc(mk.s_dve.h, 1)
        mk.s_dve.n += 1

        @blk.tensor
        def _(t):
            t.wait_ge(mk.s_pre.h, mk.s_pre.n)
            wait_evt(t, ident_ready)
            t.matmul(mk.pscan[:, 0:NH], mk.ones_f[:, :], mk.row1[0:1, 0:NH],
                     start=True, stop=True)
            t.transpose(mk.pscan[0:128, NH:NH + NC_D], mk.scr32[0:NC_D, 0:128],
                        mk.ident_f[0:NC_D, 0:NC_D])
            t.transpose(mk.pscan[0:128, 128:128 + NC_D], mk.scr32[0:NC_D, 128:256],
                        mk.ident_f[0:NC_D, 0:NC_D])
            t.transpose(mk.pscan[0:128, 192:192 + NCC], mk.scr32[0:NCC, 256:384],
                        mk.ident_f[0:NCC, 0:NCC]).then_inc(mk.s_pe.h, 1)
        mk.s_pe.n += 1

        @blk.vector
        def _(v):
            v.wait_ge(mk.s_pe.h, mk.s_pe.n)
            v.tensor_copy(mk.Dbc[:, :], mk.pscan[:, 0:NH])
            v.tensor_copy(mk.lnw_c[:, :], mk.pscan[0:128, NH:NH + NC_D])
            v.tensor_copy(mk.nw_c[:, :], mk.pscan[0:128, 128:128 + NC_D])
            v.tensor_copy(mk.mup_fm[:, 0:NCC], mk.pscan[0:128, 192:192 + NCC]).then_inc(mk.s_dve.h, 1)
        mk.s_dve.n += 1
        self.pscan_free = mk.s_dve.now()

    def p1_layernorm(self):
        mk = self
        blk = self.block
        tiles = [(0, 0, 3)] + [(1 + t, 3 + 128 * t, 128) for t in range(NT)]
        xbuf_free = {0: None, 1: None}
        xn_free = None
        ptr_free = [None, None]

        for (ti, lo, n) in tiles:
            bi = ti % 2
            xt = mk.S32a if bi == 0 else mk.S32b

            @blk.sync
            def _(sync, lo=lo, n=n, xt=xt, w=xbuf_free[bi]):
                wait_evt(sync, w)
                sync.dma_start(out=xt[0:n, :], in_=mk.x_in[lo:lo + n, :]).then_inc(mk.s_ld.h, 16)
            mk.s_ld.n += 16

            @blk.vector
            def _(v, n=n, xt=xt, w=xn_free):
                v.wait_ge(mk.s_ld.h, mk.s_ld.n)
                wait_evt(v, w)
                v.tensor_scalar(out=mk.Sbf_a[0:n, :], in0=xt[0:n, :], scalar1=0.0,
                                scalar2=None, op0=OP.add, op1=OP.add,
                                accum_out=mk.stat[0:n, 0:1])
                v.drain()
                v.scalar_tensor_tensor(out=mk.Sbf_a[0:n, :], in0=xt[0:n, :], scalar=1.0,
                                       in1=xt[0:n, :], op0=OP.mult, op1=OP.mult,
                                       accum_out=mk.stat[0:n, 1:2])
                v.drain()
                v.tensor_scalar(out=mk.stat[0:n, 2:3], in0=mk.stat[0:n, 0:1],
                                scalar1=1.0 / 4096, scalar2=None, op0=OP.mult, op1=OP.bypass)
                v.drain()
                v.tensor_tensor(out=mk.stat[0:n, 3:4], in0=mk.stat[0:n, 2:3],
                                in1=mk.stat[0:n, 2:3], op=OP.mult)
                v.drain()
                v.tensor_scalar(out=mk.stat[0:n, 4:5], in0=mk.stat[0:n, 1:2],
                                scalar1=1.0 / 4096, scalar2=mk.stat[0:n, 3:4],
                                op0=OP.mult, op1=OP.subtract)
                v.drain()
                v.tensor_scalar(out=mk.stat[0:n, 5:6], in0=mk.stat[0:n, 4:5], scalar1=EPS,
                                scalar2=None, op0=OP.add, op1=OP.bypass)
                v.drain()
                v.reciprocal(mk.stat[0:n, 6:7], mk.stat[0:n, 5:6])
                v.drain()
                v.tensor_tensor(out=mk.stat[0:n, 7:8], in0=mk.stat[0:n, 4:5],
                                in1=mk.stat[0:n, 6:7], op=OP.mult)
                v.drain()
                v.tensor_scalar(out=mk.stat[0:n, 8:9], in0=mk.stat[0:n, 7:8], scalar1=EPS,
                                scalar2=None, op0=OP.add, op1=OP.bypass)
                v.drain()
                v.reciprocal(mk.stat[0:n, 9:10], mk.stat[0:n, 8:9])
                v.engine_nop().then_inc(mk.s_dve.h, 1)
            mk.s_dve.n += 1

            @blk.scalar
            def _(sc, ti=ti, n=n):
                sc.wait_ge(mk.s_dve.h, mk.s_dve.n)
                sc.activation(out=mk.stat[0:n, 10:11], in_=mk.stat[0:n, 6:7], func=AF.Sqrt)
                sc.activation(out=mk.r2col[0:n, ti:ti + 1], in_=mk.stat[0:n, 9:10],
                              func=AF.Sqrt).then_inc(mk.s_act.h, 1)
            mk.s_act.n += 1

            @blk.vector
            def _(v, n=n, xt=xt):
                v.wait_ge(mk.s_act.h, mk.s_act.n)
                v.tensor_scalar(out=mk.Sbf_a[0:n, :], in0=xt[0:n, :],
                                scalar1=mk.stat[0:n, 2:3], scalar2=mk.stat[0:n, 10:11],
                                op0=OP.subtract, op1=OP.mult)
                v.engine_nop().then_inc(mk.s_dve.h, 1)
            mk.s_dve.n += 1
            xbuf_free[bi] = mk.s_dve.now()
            xn_ready = mk.s_dve.n

            evac_evts = []

            @blk.tensor
            def _(t, ti=ti, n=n, xn_ready=xn_ready, pf=list(ptr_free)):
                t.wait_ge(mk.s_dve.h, xn_ready)
                for batch in range(4):
                    bank = mk.ptr[batch % 2]
                    w = pf[batch % 2] if batch < 2 else (mk.s_dve, xn_ready + (batch - 1))
                    wait_evt(t, w)
                    last = None
                    for j in range(8):
                        c = batch * 8 + j
                        last = t.transpose(bank[0:128, j * 128: j * 128 + n],
                                           mk.Sbf_a[0:n, c * 128:(c + 1) * 128],
                                           mk.ident_b[0:n, 0:n])
                    last.then_inc(mk.s_pe.h, 1)
                wait_evt(t, self.pscan_free)
                t.transpose(mk.pscan[0:1, 128:128 + n], mk.r2col[0:n, ti:ti + 1],
                            mk.ident_f[0:n, 0:n]).then_inc(mk.s_pe.h, 1)
            pe_pre = mk.s_pe.n
            mk.s_pe.n += 5
            xn_free = (mk.s_pe, pe_pre + 4)

            @blk.vector
            def _(v, lo=lo, n=n, pe_pre=pe_pre):
                for batch in range(4):
                    bank = mk.ptr[batch % 2]
                    v.wait_ge(mk.s_pe.h, pe_pre + batch + 1)
                    last = None
                    for j in range(8):
                        c = batch * 8 + j
                        last = v.tensor_copy(mk.xnT[:, c * XNW + lo: c * XNW + lo + n],
                                             bank[0:128, j * 128: j * 128 + n])
                    last.then_inc(mk.s_dve.h, 1)
                v.wait_ge(mk.s_pe.h, pe_pre + 5)
                v.tensor_copy(mk.row1[0:1, 0:n], mk.pscan[0:1, 128:128 + n]).then_inc(mk.s_dve.h, 1)
            mk.s_dve.n += 5
            ptr_free[0] = (mk.s_dve, mk.s_dve.n - 2)
            ptr_free[1] = (mk.s_dve, mk.s_dve.n - 1)

            @blk.tensor
            def _(t, n=n):
                t.wait_ge(mk.s_dve.h, mk.s_dve.n)
                t.matmul(mk.pscan[:, 256:256 + n], mk.ones_f[:, :], mk.row1[0:1, 0:n],
                         start=True, stop=True).then_inc(mk.s_pe.h, 1)
            mk.s_pe.n += 1

            @blk.vector
            def _(v, lo=lo, n=n):
                v.wait_ge(mk.s_pe.h, mk.s_pe.n)
                v.tensor_copy(mk.r2bc[:, lo:lo + n], mk.pscan[:, 256:256 + n]).then_inc(mk.s_dve.h, 1)
            mk.s_dve.n += 1
            self.pscan_free = mk.s_dve.now()

        self.ln_done_dve = mk.s_dve.now()
        self.ptr_free = ptr_free
        if self.debug and self.stage == 1:
            self.dbg_items += [("xnT", self.xnT), ("r2bc", self.r2bc), ("r2col", self.r2col)]

    def w_unit_load(self, uidx, src_ap, ncols, scale_cols=None, pre=None):
        mk = self
        blk = self.block
        slot = uidx % 3
        base = slot * 8192
        w_free = self.unit_done.get(uidx - 3)

        @blk.gpsimd
        def _(g):
            wait_evt(g, w_free)
            wait_evt(g, pre)
            g.dma_start(out=mk.Wbuf[:, base:base + 16 * ncols].rearrange("p (kc c) -> p kc c", c=ncols),
                        in_=src_ap).then_inc(mk.s_w.h, 16)
        mk.s_w.n += 16
        ready = (mk.s_w, mk.s_w.n)

        if scale_cols is not None:
            col_t, base_chunk = scale_cols

            @blk.vector
            def _(v, ready=ready):
                wait_evt(v, ready)
                last = None
                for kc in range(16):
                    last = v.tensor_scalar(
                        out=mk.Wbuf[:, base + kc * ncols: base + (kc + 1) * ncols],
                        in0=mk.Wbuf[:, base + kc * ncols: base + (kc + 1) * ncols],
                        scalar1=col_t[:, base_chunk + kc: base_chunk + kc + 1],
                        scalar2=None, op0=OP.mult, op1=OP.bypass)
                last.then_inc(mk.s_wsc.h, 1)
            mk.s_wsc.n += 1
            ready = (mk.s_wsc, mk.s_wsc.n)
        return base, ready

    def p2_inproj(self):
        mk = self
        blk = self.block
        self.uidx = 0
        uidx = 0
        bank_free = {i: self.ln_done_dve for i in range(3)}
        paux_free = None
        mupbc_free = None
        zw_cons = []
        ptr_free = list(self.ptr_free)
        ubuf_free = None
        xbc_slot_free = [None] * 2
        convacc_free = None
        xsfm_free = [None, None]

        for p in range(8):
            units = []
            for h in range(2):
                src = mk.Wp[2048 * h: 2048 * (h + 1), 512 * p: 512 * (p + 1)].rearrange(
                    "(kc p) c -> p kc c", p=128)
                units.append(self.w_unit_load(uidx, src, 512, (mk.lnw_c, 16 * h)))
                uidx += 1

            @blk.gpsimd
            def _(g, p=p, w=mupbc_free):
                wait_evt(g, w)
                g.dma_start(out=mk.brow[0:1, 0:512],
                            in_=mk.mup[512 * p: 512 * (p + 1)].rearrange("(o c) -> o c", o=1)
                            ).then_inc(mk.s_w.h, 16)
            mk.s_w.n += 16

            @blk.tensor
            def _(t, w=self.pscan_free, wn=mk.s_w.n):
                t.wait_ge(mk.s_w.h, wn)
                wait_evt(t, w)
                t.matmul(mk.pscan[:, 0:512], mk.ones_b[:, :], mk.brow[0:1, 0:512],
                         start=True, stop=True).then_inc(mk.s_pe.h, 1)
            mk.s_pe.n += 1

            @blk.vector
            def _(v, w=mupbc_free):
                v.wait_ge(mk.s_pe.h, mk.s_pe.n)
                wait_evt(v, w)
                v.tensor_copy(mk.hsbuf[:, 0:512], mk.pscan[:, 0:512]).then_inc(mk.s_dve.h, 1)
            mk.s_dve.n += 1
            self.pscan_free = mk.s_dve.now()
            mupbc_ready = mk.s_dve.now()

            for t_i in range(NT):
                bank_i = (p * NT + t_i) % 3
                bank = mk.pb[bank_i]

                @blk.tensor
                def _(t, p=p, t_i=t_i, bank=bank, units=units,
                      bf=bank_free[bank_i], mr=mupbc_ready):
                    wait_evts(t, bf, mr)
                    last = None
                    for h in range(2):
                        wait_evt(t, units[h][1])
                        ub = units[h][0]
                        for kc in range(16):
                            last = t.matmul(bank[:, :],
                                            mk.xnT_sl(16 * h + kc, 3 + 128 * t_i, 128),
                                            mk.Wbuf[:, ub + kc * 512: ub + (kc + 1) * 512],
                                            start=(h == 0 and kc == 0), stop=(h == 1 and kc == 15))
                    last.then_inc(mk.s_pe.h, 1)
                mk.s_pe.n += 1

                if p == 0:
                    @blk.tensor
                    def _(t, t_i=t_i, units=units, w=paux_free):
                        wait_evt(t, w)
                        for h in range(2):
                            for kc in range(16):
                                t.matmul(mk.paux[:, 8:16],
                                         mk.xnT_sl(16 * h + kc, 3 + 128 * t_i, 128),
                                         mk.Wg_sb[:, (16 * h + kc) * 8: (16 * h + kc + 1) * 8],
                                         start=(h == 0 and kc == 0), stop=False)
                        t.matmul(mk.paux[:, 8:16], mk.ones_b[:, :], mk.brow[0:1, 512:520],
                                 start=False, stop=True).then_inc(mk.s_pe.h, 1)
                    mk.s_pe.n += 1

                    @blk.vector
                    def _(v, t_i=t_i):
                        v.wait_ge(mk.s_pe.h, mk.s_pe.n)
                        msl = mk.masks[:, 8 * t_i: 8 * t_i + 8]
                        v.tensor_reduce(out=mk.mx[:, 0:1], in_=mk.paux[:, 8:16],
                                        axis=mybir.AxisListType.X, op=OP.max)
                        v.drain()
                        v.tensor_scalar(out=msl, in0=mk.paux[:, 8:16], scalar1=mk.mx[:, 0:1],
                                        scalar2=-1e30, op0=OP.is_equal, op1=OP.mult)
                        v.drain()
                        v.tensor_tensor(out=msl, in0=msl, in1=mk.paux[:, 8:16], op=OP.add)
                        v.drain()
                        v.tensor_reduce(out=mk.mx[:, 1:2], in_=msl,
                                        axis=mybir.AxisListType.X, op=OP.max)
                        v.drain()
                        v.tensor_scalar(out=msl, in0=mk.paux[:, 8:16], scalar1=mk.mx[:, 1:2],
                                        scalar2=0.5, op0=OP.is_ge, op1=OP.mult).then_inc(mk.s_dve.h, 1)
                    mk.s_dve.n += 1
                    paux_free = mk.s_dve.now()

                zidx = p * NT + t_i

                @blk.vector
                def _(v, p=p, t_i=t_i, bank=bank,
                      w=(zw_cons[zidx - 2] if zidx >= 2 else None)):
                    v.wait_ge(mk.s_pe.h, mk.s_pe.n)
                    wait_evt(v, w)
                    v.scalar_tensor_tensor(out=mk.Sbf_b[:, 2560 + 512 * (zidx % 2): 3072 + 512 * (zidx % 2)],
                                           in0=bank[:, :], scalar=mk.r2col[:, 1 + t_i: 2 + t_i],
                                           in1=mk.hsbuf[:, 0:512], op0=OP.mult,
                                           op1=OP.mult).then_inc(mk.s_dve.h, 1)
                mk.s_dve.n += 1
                bank_free[bank_i] = mk.s_dve.now()
                mupbc_free = mk.s_dve.now()

                @blk.sync
                def _(sync, p=p, t_i=t_i, zidx=zidx):
                    sync.wait_ge(mk.s_dve.h, mk.s_dve.n)
                    sync.dma_start(out=mk.z_scr[128 * t_i: 128 * (t_i + 1), 512 * p: 512 * (p + 1)],
                                   in_=mk.Sbf_b[:, 2560 + 512 * (zidx % 2): 3072 + 512 * (zidx % 2)]
                                   ).then_inc(mk.s_st.h, 16)
                mk.s_st.n += 16
                zw_cons.append((mk.s_st, mk.s_st.n))

            self.unit_done[uidx - 2] = (mk.s_pe, mk.s_pe.n)
            self.unit_done[uidx - 1] = (mk.s_pe, mk.s_pe.n)

        ccount = 0
        for p in range(9):
            ncols = 512 if p < 8 else 320
            c0 = 4096 + 512 * p
            units = []
            for h in range(2):
                src = mk.Wp[2048 * h: 2048 * (h + 1), c0: c0 + ncols].rearrange(
                    "(kc p) c -> p kc c", p=128)
                units.append(self.w_unit_load(uidx, src, ncols, (mk.lnw_c, 16 * h)))
                uidx += 1
            nchunk = 4 if p < 8 else 2
            for j in range(nchunk):
                cg = 4 * p + j
                bank_a = ccount % 3
                bank_b = (ccount + 1) % 3
                ccount += 2

                @blk.tensor
                def _(t, j=j, cg=cg, units=units, ncols=ncols, bank_a=bank_a, bank_b=bank_b,
                      bfa=bank_free[bank_a], bfb=bank_free[bank_b], pxf=paux_free):
                    wait_evts(t, bfa, bfb, pxf)
                    last = None
                    for h in range(2):
                        wait_evt(t, units[h][1])
                        ub = units[h][0]
                        for kc in range(16):
                            w_sl = mk.Wbuf[:, ub + kc * ncols + 128 * j: ub + kc * ncols + 128 * (j + 1)]
                            st = (h == 0 and kc == 0)
                            sp = (h == 1 and kc == 15)
                            t.matmul(mk.pb[bank_a][:, :], w_sl, mk.xnT_sl(16 * h + kc, 0, 512),
                                     start=st, stop=sp)
                            t.matmul(mk.pb[bank_b][:, :], w_sl, mk.xnT_sl(16 * h + kc, 512, 512),
                                     start=st, stop=sp)
                            last = t.matmul(mk.paux[:, 0:3], w_sl,
                                            mk.xnT_sl(16 * h + kc, 1024, 3),
                                            start=st, stop=sp)
                    last.then_inc(mk.s_pe.h, 1)
                mk.s_pe.n += 1

                @blk.vector
                def _(v, cg=cg, bank_a=bank_a, bank_b=bank_b, w=xbc_slot_free[cg % 2]):
                    v.wait_ge(mk.s_pe.h, mk.s_pe.n)
                    wait_evt(v, w)
                    sl = mk.Sbf_b[:, XNW * (cg % 2): XNW * (cg % 2) + XNW]
                    mcol = mk.mup_fm[:, cg: cg + 1]
                    v.scalar_tensor_tensor(out=sl[:, 0:512], in0=mk.pb[bank_a][:, :],
                                           scalar=mcol, in1=mk.r2bc[:, 0:512],
                                           op0=OP.mult, op1=OP.mult)
                    v.scalar_tensor_tensor(out=sl[:, 512:1024], in0=mk.pb[bank_b][:, :],
                                           scalar=mcol, in1=mk.r2bc[:, 512:1024],
                                           op0=OP.mult, op1=OP.mult)
                    v.scalar_tensor_tensor(out=sl[:, 1024:1027], in0=mk.paux[:, 0:3],
                                           scalar=mcol, in1=mk.r2bc[:, 1024:1027],
                                           op0=OP.mult, op1=OP.mult).then_inc(mk.s_dve.h, 1)
                mk.s_dve.n += 1
                bank_free[bank_a] = mk.s_dve.now()
                bank_free[bank_b] = mk.s_dve.now()
                paux_free = mk.s_dve.now()

                @blk.vector
                def _(v, cg=cg, w=convacc_free):
                    wait_evt(v, w)
                    sl = mk.Sbf_b[:, XNW * (cg % 2): XNW * (cg % 2) + XNW]
                    v.tensor_scalar(out=mk.scr32[:, :], in0=sl[:, 0:1024],
                                    scalar1=mk.convw[:, 4 * cg: 4 * cg + 1], scalar2=None,
                                    op0=OP.mult, op1=OP.bypass)
                    for k in range(1, 4):
                        v.drain()
                        v.scalar_tensor_tensor(out=mk.scr32[:, :], in0=sl[:, k:k + 1024],
                                               scalar=mk.convw[:, 4 * cg + k: 4 * cg + k + 1],
                                               in1=mk.scr32[:, :], op0=OP.mult, op1=OP.add)
                    v.engine_nop().then_inc(mk.s_dve.h, 1)
                mk.s_dve.n += 1
                xbc_slot_free[cg % 2] = (mk.s_dve, mk.s_dve.n)
                conv_dve = mk.s_dve.n

                @blk.scalar
                def _(sc, cg=cg, dn=conv_dve, w=xsfm_free[cg % 2] if cg < 32 else None):
                    sc.wait_ge(mk.s_dve.h, dn)
                    wait_evt(sc, w)
                    if cg < 32:
                        dst = mk.Sbf_a[:, 1024 * (cg % 2): 1024 * (cg % 2) + 1024]
                    elif cg == 32:
                        dst = mk.B_fm[:, 3:1027]
                    else:
                        dst = mk.C_fm[:, 3:1027]
                    sc.activation(out=dst, in_=mk.scr32[:, :], func=AF.Silu).then_inc(mk.s_act.h, 1)
                mk.s_act.n += 1
                convacc_free = mk.s_act.now()

                if cg < 32:
                    @blk.tensor
                    def _(t, cg=cg, pf=list(ptr_free)):
                        t.wait_ge(mk.s_act.h, mk.s_act.n)
                        src = mk.Sbf_a[:, 1024 * (cg % 2): 1024 * (cg % 2) + 1024]
                        for half in range(2):
                            bank = mk.ptr[half]
                            wait_evt(t, pf[half])
                            last = None
                            for j4 in range(4):
                                tt = half * 4 + j4
                                last = t.transpose(bank[0:128, j4 * 128: (j4 + 1) * 128],
                                                   src[:, tt * 128: (tt + 1) * 128],
                                                   mk.ident_b[:, :])
                            last.then_inc(mk.s_pe.h, 1)
                    pe_pre = mk.s_pe.n
                    mk.s_pe.n += 2
                    xsfm_free[cg % 2] = (mk.s_pe, mk.s_pe.n)

                    @blk.vector
                    def _(v, cg=cg, pe_pre=pe_pre, w=ubuf_free):
                        for half in range(2):
                            bank = mk.ptr[half]
                            v.wait_ge(mk.s_pe.h, pe_pre + half + 1)
                            if half == 0:
                                wait_evt(v, w)
                            last = None
                            for j4 in range(4):
                                tt = half * 4 + j4
                                last = v.tensor_copy(mk.Sbf_a[:, 2048 + 128 * tt: 2048 + 128 * (tt + 1)],
                                                     bank[0:128, j4 * 128: (j4 + 1) * 128])
                            last.then_inc(mk.s_dve.h, 1)
                    mk.s_dve.n += 2
                    ptr_free[0] = (mk.s_dve, mk.s_dve.n - 1)
                    ptr_free[1] = (mk.s_dve, mk.s_dve.n)

                    @blk.sync
                    def _(sync, cg=cg):
                        sync.wait_ge(mk.s_dve.h, mk.s_dve.n)
                        for tt in range(8):
                            sync.dma_start(
                                out=mk.xs_scr[128 * tt: 128 * (tt + 1), 128 * cg: 128 * (cg + 1)],
                                in_=mk.Sbf_a[:, 2048 + 128 * tt: 2048 + 128 * (tt + 1)]).then_inc(mk.s_st.h, 16)
                    mk.s_st.n += 8 * 16
                    ubuf_free = (mk.s_st, mk.s_st.n)

            if p < 8:
                self.unit_done[uidx - 2] = (mk.s_pe, mk.s_pe.n)
                self.unit_done[uidx - 1] = (mk.s_pe, mk.s_pe.n)

        for g2 in range(2):
            bank_i = g2 % 3

            @blk.tensor
            def _(t, g2=g2, units=units, bf=bank_free[bank_i], bank_i=bank_i):
                wait_evt(t, bf)
                last = None
                for h in range(2):
                    wait_evt(t, units[h][1])
                    ub = units[h][0]
                    for kc in range(16):
                        w_sl = mk.Wbuf[:, ub + kc * 320 + 256: ub + kc * 320 + 320]
                        last = t.matmul(mk.pb[bank_i][0:NH, :], w_sl,
                                        mk.xnT_sl(16 * h + kc, 3 + 512 * g2, 512),
                                        start=(h == 0 and kc == 0), stop=(h == 1 and kc == 15))
                last.then_inc(mk.s_pe.h, 1)
            mk.s_pe.n += 1

            @blk.vector
            def _(v, g2=g2, bank_i=bank_i):
                v.wait_ge(mk.s_pe.h, mk.s_pe.n)
                v.scalar_tensor_tensor(out=mk.cum[:, 512 * g2: 512 * (g2 + 1)],
                                       in0=mk.pb[bank_i][0:NH, :], scalar=mk.mup_dt[:, 0:1],
                                       in1=mk.r2bc[0:NH, 3 + 512 * g2: 3 + 512 * (g2 + 1)],
                                       op0=OP.mult, op1=OP.mult).then_inc(mk.s_dve.h, 1)
            mk.s_dve.n += 1
            bank_free[bank_i] = mk.s_dve.now()
        self.unit_done[uidx - 2] = (mk.s_pe, mk.s_pe.n)
        self.unit_done[uidx - 1] = (mk.s_pe, mk.s_pe.n)

        @blk.tensor
        def _(t, pf=list(ptr_free)):
            t.wait_ge(mk.s_act.h, mk.s_act.n)
            for half in range(2):
                bank = mk.ptr[half]
                wait_evt(t, pf[half])
                last = None
                for j4 in range(4):
                    tt = half * 4 + j4
                    last = t.transpose(bank[0:128, j4 * 128: (j4 + 1) * 128],
                                       mk.B_fm[:, 3 + tt * 128: 3 + (tt + 1) * 128],
                                       mk.ident_b[:, :])
                last.then_inc(mk.s_pe.h, 1)
        pe_pre = mk.s_pe.n
        mk.s_pe.n += 2

        @blk.vector
        def _(v, pe_pre=pe_pre):
            for half in range(2):
                bank = mk.ptr[half]
                v.wait_ge(mk.s_pe.h, pe_pre + half + 1)
                last = None
                for j4 in range(4):
                    tt = half * 4 + j4
                    last = v.tensor_copy(mk.B_tok[:, 128 * tt: 128 * (tt + 1)],
                                         bank[0:128, j4 * 128: (j4 + 1) * 128])
                last.then_inc(mk.s_dve.h, 1)
        mk.s_dve.n += 2
        ptr_free[0] = (mk.s_dve, mk.s_dve.n - 1)
        ptr_free[1] = (mk.s_dve, mk.s_dve.n)
        self.ptr_free = ptr_free

        for k in range(8):
            @blk.tensor
            def _(t, k=k, w=self.pscan_free):
                wait_evt(t, w)
                t.matmul(mk.pscan[:, 0:128], mk.B_fm[:, 3 + 128 * k: 3 + 128 * (k + 1)],
                         mk.C_fm[:, 3 + 128 * k: 3 + 128 * (k + 1)],
                         start=True, stop=True).then_inc(mk.s_pe.h, 1)
            mk.s_pe.n += 1

            @blk.vector
            def _(v, k=k):
                v.wait_ge(mk.s_pe.h, mk.s_pe.n)
                v.tensor_copy(mk.GT_all[:, 128 * k: 128 * (k + 1)],
                              mk.pscan[:, 0:128]).then_inc(mk.s_dve.h, 1)
            mk.s_dve.n += 1
            self.pscan_free = mk.s_dve.now()

        self.bank_free = bank_free
        self.paux_free2 = paux_free
        self.uidx = uidx
        if self.debug and self.stage == 2:
            self.dbg_items += [("B_fm", self.B_fm), ("C_fm", self.C_fm), ("masks", self.masks),
                               ("dt_raw", self.cum), ("B_tok", self.B_tok),
                               ("GT_all", self.GT_all)]

    def p3_dtchain(self):
        mk = self
        blk = self.block

        @blk.scalar
        def _(sc, dve_need=mk.s_dve.n):
            sc.wait_ge(mk.s_dve.h, dve_need)
            sc.activation(out=mk.dt_sb[:, :], in_=mk.cum[:, :], func=AF.Abs,
                          bias=mk.dtbcol[:, 0:1])
            sc.drain()
            sc.activation(out=mk.dt_sb[:, :], in_=mk.dt_sb[:, :], func=AF.Exp, scale=-1.0)
            sc.drain()
            sc.activation(out=mk.dt_sb[:, :], in_=mk.dt_sb[:, :], func=AF.Ln,
                          bias=1.0).then_inc(mk.s_act.h, 1)
        mk.s_act.n += 1

        @blk.vector
        def _(v):
            v.wait_ge(mk.s_act.h, mk.s_act.n)
            v.tensor_scalar(out=mk.cs2[:, :], in0=mk.cum[:, :], scalar1=mk.dtbcol[:, 0:1],
                            scalar2=0.0, op0=OP.add, op1=OP.max)
            v.drain()
            v.tensor_tensor(out=mk.dt_sb[:, :], in0=mk.cs2[:, :], in1=mk.dt_sb[:, :], op=OP.add)
            v.engine_nop().then_inc(mk.s_dve.h, 1)
        mk.s_dve.n += 1

        @blk.scalar
        def _(sc):
            sc.wait_ge(mk.s_dve.h, mk.s_dve.n)
            sc.activation(out=mk.cs2[:, :], in_=mk.dt_sb[:, :], func=AF.Ln).then_inc(mk.s_act.h, 1)
        mk.s_act.n += 1

        @blk.vector
        def _(v):
            v.wait_ge(mk.s_act.h, mk.s_act.n)
            v.tensor_scalar(out=mk.scr32[0:NH, :], in0=mk.dt_sb[:, :], scalar1=mk.Acol[:, 0:1],
                            scalar2=None, op0=OP.mult, op1=OP.bypass)
            v.drain()
            for k in range(8):
                v.tensor_tensor_scan(out=mk.cum[:, 128 * k: 128 * (k + 1)],
                                     data0=mk.scr32[0:NH, 128 * k: 128 * (k + 1)],
                                     data1=mk.zeros64[:, :], initial=0.0,
                                     op0=OP.add, op1=OP.add)
            v.drain()
            v.tensor_tensor(out=mk.cs2[:, :], in0=mk.cum[:, :], in1=mk.cs2[:, :],
                            op=OP.subtract).then_inc(mk.s_dve.h, 1)
        mk.s_dve.n += 1

        @blk.sync
        def _(sync):
            sync.wait_ge(mk.s_dve.h, mk.s_dve.n)
            sync.dma_start(out=mk.cum_scr[:, :], in_=mk.cum[:, :]).then_inc(mk.s_st.h, 16)
        mk.s_st.n += 16
        self.cum_scr_ready = (mk.s_st, mk.s_st.n)

        self.p3_done_dve = mk.s_dve.now()
        if self.debug and self.stage == 3:
            self.dbg_items += [("dt_sb", self.dt_sb), ("cum", self.cum), ("cs2", self.cs2)]

    def p4_passA(self):
        mk = self
        blk = self.block

        @blk.gpsimd
        def _(g, w=self.ln_done_dve):
            wait_evt(g, w)
            g.memset(mk.S32b[:, :], 0.0)
            g.engine_nop().then_inc(mk.s_pool.h, 1)
        mk.s_pool.n += 1

        xs_all_stored = (mk.s_st, mk.s_st.n)
        sbfa_free = None
        sbfb_free = None
        wq_free = None

        for k in range(8):
            @blk.sync
            def _(sync, k=k, w1=xs_all_stored, w2=sbfa_free):
                wait_evts(sync, w1, w2)
                sync.dma_start(out=mk.Sbf_a[:, :],
                               in_=mk.xs_scr[128 * k: 128 * (k + 1), :]).then_inc(mk.s_ld.h, 16)
            mk.s_ld.n += 16
            xs_ld = mk.s_ld.n

            @blk.vector
            def _(v, k=k, w=wq_free):
                wait_evt(v, w)
                v.tensor_scalar(out=mk.wq[:, 0:T], in0=mk.cum[:, 128 * k: 128 * (k + 1)],
                                scalar1=mk.cum[:, 128 * k + 127: 128 * k + 128],
                                scalar2=None, op0=OP.subtract, op1=OP.bypass).then_inc(mk.s_dve.h, 1)
            mk.s_dve.n += 1

            @blk.scalar
            def _(sc, k=k):
                sc.wait_ge(mk.s_dve.h, mk.s_dve.n)
                sc.activation(out=mk.wq[:, 0:T], in_=mk.wq[:, 0:T], func=AF.Exp,
                              scale=-1.0)
                sc.drain()
                sc.activation(out=mk.Pcol[:, 1:2], in_=mk.cum[:, 128 * k + 127: 128 * k + 128],
                              func=AF.Exp).then_inc(mk.s_act.h, 1)
            mk.s_act.n += 1

            @blk.vector
            def _(v, k=k):
                v.wait_ge(mk.s_act.h, mk.s_act.n)
                v.tensor_tensor(out=mk.wq[:, 0:T], in0=mk.dt_sb[:, 128 * k: 128 * (k + 1)],
                                in1=mk.wq[:, 0:T], op=OP.mult)
                v.drain()
                v.tensor_tensor(out=mk.Pcol[:, 0:1], in0=mk.Pcol[:, 0:1],
                                in1=mk.Pcol[:, 1:2], op=OP.mult).then_inc(mk.s_dve.h, 1)
            mk.s_dve.n += 1

            @blk.tensor
            def _(t, k=k, w=self.pscan_free):
                t.wait_ge(mk.s_dve.h, mk.s_dve.n)
                wait_evt(t, w)
                t.transpose(mk.pscan[0:128, 0:NH], mk.wq[:, 0:T],
                            mk.ident_f[0:NH, 0:NH]).then_inc(mk.s_pe.h, 1)
                t.transpose(mk.pscan[0:1, 256:256 + NH],
                            mk.cum[:, 128 * k + 127: 128 * k + 128],
                            mk.ident_f[0:NH, 0:NH]).then_inc(mk.s_pe.h, 1)
            mk.s_pe.n += 2
            wq_free = (mk.s_pe, mk.s_pe.n)

            @blk.vector
            def _(v, k=k):
                v.wait_ge(mk.s_pe.h, mk.s_pe.n)
                v.tensor_copy(mk.wqT[:, :], mk.pscan[0:128, 0:NH])
                v.tensor_copy(mk.row1[0:1, 0:NH], mk.pscan[0:1, 256:256 + NH]).then_inc(mk.s_dve.h, 1)
            mk.s_dve.n += 1

            @blk.scalar
            def _(sc):
                sc.wait_ge(mk.s_dve.h, mk.s_dve.n)
                sc.activation(out=mk.row1[0:1, 0:NH], in_=mk.row1[0:1, 0:NH],
                              func=AF.Exp).then_inc(mk.s_act.h, 1)
            mk.s_act.n += 1

            @blk.tensor
            def _(t):
                t.wait_ge(mk.s_act.h, mk.s_act.n)
                t.matmul(mk.pscan[:, 256:256 + NH], mk.ones_f[:, :], mk.row1[0:1, 0:NH],
                         start=True, stop=True).then_inc(mk.s_pe.h, 1)
            mk.s_pe.n += 1

            @blk.vector
            def _(v, k=k):
                v.wait_ge(mk.s_pe.h, mk.s_pe.n)
                v.tensor_copy(mk.aTbc[:, :], mk.pscan[:, 256:256 + NH])
                v.drain()
                if k < 7:
                    v.tensor_tensor(out=mk.Qbc[:, NH * (k + 1): NH * (k + 2)],
                                    in0=mk.Qbc[:, NH * k: NH * (k + 1)], in1=mk.aTbc[:, :],
                                    op=OP.mult)
                v.engine_nop().then_inc(mk.s_dve.h, 1)
            mk.s_dve.n += 1
            self.pscan_free = mk.s_dve.now()

            @blk.vector
            def _(v, k=k, xs_ld=xs_ld, w=sbfb_free):
                v.wait_ge(mk.s_ld.h, xs_ld)
                wait_evt(v, w)
                last = None
                for h in range(NH):
                    last = v.tensor_scalar(out=mk.Sbf_b[:, 64 * h: 64 * (h + 1)],
                                           in0=mk.Sbf_a[:, 64 * h: 64 * (h + 1)],
                                           scalar1=mk.wqT[:, h: h + 1], scalar2=None,
                                           op0=OP.mult, op1=OP.bypass)
                last.then_inc(mk.s_dve.h, 1)
            mk.s_dve.n += 1
            sbfa_free = (mk.s_dve, mk.s_dve.n)
            xw_ready = mk.s_dve.n

            pY_free = None
            for p in range(8):
                @blk.tensor
                def _(t, k=k, p=p, xw_ready=xw_ready, w=pY_free):
                    t.wait_ge(mk.s_dve.h, xw_ready)
                    wait_evt(t, w)
                    t.matmul(mk.pY[:, :], mk.B_tok[:, 128 * k: 128 * (k + 1)],
                             mk.Sbf_b[:, 512 * p: 512 * (p + 1)],
                             start=True, stop=True).then_inc(mk.s_pe.h, 1)
                mk.s_pe.n += 1

                @blk.vector
                def _(v, k=k, p=p):
                    v.wait_ge(mk.s_pe.h, mk.s_pe.n)
                    last = None
                    for hh in range(8):
                        h = 8 * p + hh
                        sl = mk.S32b[:, 64 * h: 64 * (h + 1)]
                        last = v.scalar_tensor_tensor(out=sl, in0=mk.pY[:, 64 * hh: 64 * (hh + 1)],
                                                      scalar=1.0, in1=sl, op0=OP.mult, op1=OP.add)
                    v.drain()
                    for hh in range(8):
                        h = 8 * p + hh
                        sl = mk.S32b[:, 64 * h: 64 * (h + 1)]
                        last = v.tensor_scalar(out=sl, in0=sl, scalar1=mk.aTbc[:, h: h + 1],
                                               scalar2=None, op0=OP.mult, op1=OP.bypass)
                    last.then_inc(mk.s_dve.h, 1)
                mk.s_dve.n += 1
                pY_free = (mk.s_dve, mk.s_dve.n)
            sbfb_free = (mk.s_pe, mk.s_pe.n)

            if k < 7:
                @blk.gpsimd
                def _(g, k=k):
                    g.wait_ge(mk.s_dve.h, mk.s_dve.n)
                    g.dma_start(out=mk.sloc_scr[k, :, :], in_=mk.S32b[:, :]).then_inc(mk.s_gst.h, 16)
                mk.s_gst.n += 16

        @blk.gpsimd
        def _(g, dve_need=mk.s_dve.n):
            g.wait_ge(mk.s_dve.h, dve_need)
            g.dma_start(out=mk.ag_in[:, :], in_=mk.S32b[:, :]).then_inc(mk.s_gst.h, 16)
            g.dma_start(out=mk.agp_in[:, :], in_=mk.Pcol[:, 0:1]).then_inc(mk.s_gst.h, 16)
            g.wait_ge(mk.s_gst.h, mk.s_gst.n + 32)
            g.collective_compute("AllGather", OP.bypass,
                                 replica_groups=[[0, 1, 2, 3], [4, 5, 6, 7]],
                                 ins=[mk.ag_in[:, :]], outs=[mk.ag_out[:, :]]).then_inc(mk.s_cc.h)
            g.collective_compute("AllGather", OP.bypass,
                                 replica_groups=[[0, 1, 2, 3], [4, 5, 6, 7]],
                                 ins=[mk.agp_in[:, :]], outs=[mk.agp_out[:, :]]).then_inc(mk.s_cc.h)
        mk.s_gst.n += 32
        mk.s_cc.n += 2
        self.sloc_stored = (mk.s_gst, mk.s_gst.n)
        self.p4_sbfa = sbfa_free
        self.p4_sbfb = sbfb_free
        if self.debug and self.stage == 4:
            self.dbg_items += [("h_out", self.S32b), ("Qbc", self.Qbc)]

    def p5_passB(self):
        mk = self
        blk = self.block

        @blk.sync
        def _(sync, cc_need=mk.s_cc.n):
            sync.wait_ge(mk.s_cc.h, cc_need)
            sync.dma_start(out=mk.scr32[0:4, 0:NH],
                           in_=mk.agp_out.rearrange("(r h) o -> r (h o)", r=4)
                           ).then_inc(mk.s_ld.h, 16)
        mk.s_ld.n += 16

        @blk.tensor
        def _(t, w=self.pscan_free, ld=mk.s_ld.n):
            t.wait_ge(mk.s_ld.h, ld)
            wait_evt(t, w)
            t.transpose(mk.pscan[0:NH, 0:4], mk.scr32[0:4, 0:NH],
                        mk.ident_f[0:4, 0:4]).then_inc(mk.s_pe.h, 1)
        mk.s_pe.n += 1

        @blk.vector
        def _(v):
            v.wait_ge(mk.s_pe.h, mk.s_pe.n)
            v.tensor_copy(mk.gsm[:, 8:12], mk.pscan[0:NH, 0:4])
            v.drain()
            for m in range(4):
                v.tensor_tensor(out=mk.gsm[:, 12 + m:13 + m], in0=mk.gsm[:, 8 + m:9 + m],
                                in1=mk.gsm[:, 4 + m:5 + m], op=OP.mult)
            v.drain()
            for m in range(4):
                v.tensor_scalar(out=mk.wq[0:NH, m:m + 1], in0=mk.gsm[:, 4 + m:5 + m],
                                scalar1=-1.0, scalar2=1.0, op0=OP.mult, op1=OP.add)
            v.drain()
            for m in range(4):
                v.tensor_tensor(out=mk.gsm[:, 12 + m:13 + m], in0=mk.gsm[:, 12 + m:13 + m],
                                in1=mk.wq[0:NH, m:m + 1], op=OP.add)
            v.drain()
            v.memset(mk.wq[0:NH, 4:5], 1.0)
            v.drain()
            for r in (3, 2, 1, 0):
                v.tensor_tensor(out=mk.wq[0:NH, 8 + r:9 + r], in0=mk.gsm[:, 0 + r:1 + r],
                                in1=mk.wq[0:NH, 4:5], op=OP.mult)
                v.drain()
                v.tensor_tensor(out=mk.wq[0:NH, 4:5], in0=mk.wq[0:NH, 4:5],
                                in1=mk.gsm[:, 12 + r:13 + r], op=OP.mult)
                v.drain()
            v.engine_nop().then_inc(mk.s_dve.h, 1)
        mk.s_dve.n += 1

        for r in range(4):
            @blk.tensor
            def _(t, r=r, dve_need=mk.s_dve.n, w=self.pscan_free):
                t.wait_ge(mk.s_dve.h, dve_need)
                wait_evt(t, w)
                t.transpose(mk.pscan[0:1, 256:256 + NH], mk.wq[0:NH, 8 + r:9 + r],
                            mk.ident_f[0:NH, 0:NH]).then_inc(mk.s_pe.h, 1)
            mk.s_pe.n += 1

            @blk.vector
            def _(v):
                v.wait_ge(mk.s_pe.h, mk.s_pe.n)
                v.tensor_copy(mk.row1[0:1, 0:NH], mk.pscan[0:1, 256:256 + NH]).then_inc(mk.s_dve.h, 1)
            mk.s_dve.n += 1

            @blk.tensor
            def _(t):
                t.wait_ge(mk.s_dve.h, mk.s_dve.n)
                t.matmul(mk.pscan[:, 256:256 + NH], mk.ones_f[:, :], mk.row1[0:1, 0:NH],
                         start=True, stop=True).then_inc(mk.s_pe.h, 1)
            mk.s_pe.n += 1

            @blk.vector
            def _(v, r=r):
                v.wait_ge(mk.s_pe.h, mk.s_pe.n)
                v.tensor_copy(mk.wbc[:, NH * r: NH * (r + 1)],
                              mk.pscan[:, 256:256 + NH]).then_inc(mk.s_dve.h, 1)
            mk.s_dve.n += 1
            self.pscan_free = mk.s_dve.now()

        @blk.gpsimd
        def _(g, dve_need=mk.s_dve.n):
            g.wait_ge(mk.s_dve.h, dve_need)
            g.memset(mk.S32a[:, :], 0.0)
            g.engine_nop().then_inc(mk.s_pool.h, 1)
        mk.s_pool.n += 1

        hp_cons = []
        for r in range(4):
            for p in range(8):
                j = r * 8 + p

                @blk.gpsimd
                def _(g, r=r, p=p, j=j, pool_need=mk.s_pool.n,
                      w=(hp_cons[j - 2] if j >= 2 else None)):
                    g.wait_ge(mk.s_pool.h, pool_need)
                    wait_evt(g, w)
                    g.dma_start(out=mk.scr32[:, 512 * (j % 2): 512 * (j % 2) + 512],
                                in_=mk.ag_out[128 * r: 128 * (r + 1), 512 * p: 512 * (p + 1)]
                                ).then_inc(mk.s_w.h, 16)
                mk.s_w.n += 16

                @blk.vector
                def _(v, r=r, p=p, j=j, wn=mk.s_w.n):
                    v.wait_ge(mk.s_w.h, wn)
                    last = None
                    for hh in range(8):
                        h = 8 * p + hh
                        sl = mk.S32a[:, 64 * h: 64 * (h + 1)]
                        last = v.scalar_tensor_tensor(
                            out=sl,
                            in0=mk.scr32[:, 512 * (j % 2) + 64 * hh: 512 * (j % 2) + 64 * (hh + 1)],
                            scalar=mk.wbc[:, NH * r + h: NH * r + h + 1], in1=sl,
                            op0=OP.mult, op1=OP.add)
                    last.then_inc(mk.s_dve.h, 1)
                mk.s_dve.n += 1
                hp_cons.append((mk.s_dve, mk.s_dve.n))

        @blk.gpsimd
        def _(g, dve_need=mk.s_dve.n):
            g.wait_ge(mk.s_dve.h, dve_need)
            g.memset(mk.S32b[:, :], 0.0)
            g.engine_nop().then_inc(mk.s_pool.h, 1)
        mk.s_pool.n += 1
        sloc_ready = (mk.s_pool, mk.s_pool.n)

        sbfa_free = self.p4_sbfa
        sbfb_free = self.p4_sbfb
        hs_cons = [None, None]
        rows_cons = [None, None]
        cumbc_cons = [self.bank_free[2], self.paux_free2]
        exp_cons = None
        intra_done = None
        intra_bank_cons = [self.bank_free[0], self.bank_free[1]]
        pY_cons = None
        transT_cons = None
        zw_cons2 = [None, None]
        ptr_free = list(self.ptr_free)

        for k in range(8):
            @blk.gpsimd
            def _(g, k=k, w1=sbfa_free, w2=sloc_ready, w3=self.sloc_stored,
                  w4=(mk.s_st, mk.s_st.n)):
                wait_evts(g, w1, w2, w3, w4)
                if k > 0:
                    g.dma_start(out=mk.S32b[:, :],
                                in_=mk.sloc_scr[k - 1, :, :]).then_inc(mk.s_w.h, 16)
                g.dma_start(out=mk.Sbf_a[:, :],
                            in_=mk.xs_scr[128 * k: 128 * (k + 1), :]).then_inc(mk.s_w.h, 16)
            mk.s_w.n += 32 if k > 0 else 16
            chunk_ld = mk.s_w.n

            @blk.gpsimd
            def _(g, k=k):
                g.dma_start(out=mk.woutbf[512 * k: 512 * (k + 1), :],
                            in_=mk.Wout[512 * k: 512 * (k + 1), :]).then_inc(mk.s_gst.h, 16)
            mk.s_gst.n += 16

            @blk.tensor
            def _(t, k=k, w1=self.pscan_free, w2=transT_cons, w3=self.p3_done_dve):
                wait_evts(t, w1, w2, w3)
                t.transpose(mk.pscan[0:128, 256:256 + NH],
                            mk.cum[:, 128 * k: 128 * (k + 1)],
                            mk.ident_f[0:NH, 0:NH])
                t.transpose(mk.pscan[0:128, 320:320 + NH],
                            mk.cs2[:, 128 * k: 128 * (k + 1)],
                            mk.ident_f[0:NH, 0:NH]).then_inc(mk.s_pe.h, 1)
            mk.s_pe.n += 1

            @blk.vector
            def _(v, k=k):
                v.wait_ge(mk.s_pe.h, mk.s_pe.n)
                v.tensor_copy(mk.transT[:, 0:NH], mk.pscan[0:128, 256:256 + NH])
                v.tensor_scalar(out=mk.transT[:, NH:2 * NH], in0=mk.pscan[0:128, 320:320 + NH],
                                scalar1=-1.0, scalar2=None, op0=OP.mult, op1=OP.bypass).then_inc(mk.s_dve.h, 1)
            mk.s_dve.n += 1
            self.pscan_free = mk.s_dve.now()
            transT_ready = mk.s_dve.n

            @blk.scalar
            def _(sc, k=k, tr=transT_ready):
                sc.wait_ge(mk.s_dve.h, tr)
                sc.activation(out=mk.ecumT[:, :], in_=mk.transT[:, 0:NH],
                              func=AF.Exp).then_inc(mk.s_act.h, 1)
            mk.s_act.n += 1
            ecum_ready = mk.s_act.n

            for p in range(8):
                rows_ld = [0, 0]
                for gq in range(2):
                    rslot = (2 * p + gq) % 2

                    @blk.sync
                    def _(sync, k=k, p=p, gq=gq, rslot=rslot,
                          w1=self.cum_scr_ready, w2=rows_cons[rslot]):
                        wait_evts(sync, w1, w2)
                        sync.dma_start(out=mk.rows[0:1, 512 * rslot: 512 * rslot + 512]
                                       .rearrange("o (h t) -> o h t", h=4),
                                       in_=mk.cum_scr[8 * p + 4 * gq: 8 * p + 4 * (gq + 1),
                                                      128 * k: 128 * (k + 1)]
                                       .rearrange("(o h) t -> o h t", o=1)
                                       ).then_inc(mk.s_rows.h, 16)
                    mk.s_rows.n += 16
                    rows_ld[gq] = mk.s_rows.n

                hslot = p % 2

                @blk.vector
                def _(v, k=k, p=p, hslot=hslot, chunk_ld=chunk_ld, w=hs_cons[hslot]):
                    v.wait_ge(mk.s_w.h, chunk_ld)
                    wait_evt(v, w)
                    last = None
                    for hh in range(8):
                        h = 8 * p + hh
                        last = v.scalar_tensor_tensor(
                            out=mk.hsbuf[:, 512 * hslot + 64 * hh: 512 * hslot + 64 * (hh + 1)],
                            in0=mk.S32a[:, 64 * h: 64 * (h + 1)],
                            scalar=mk.Qbc[:, NH * k + h: NH * k + h + 1],
                            in1=mk.S32b[:, 64 * h: 64 * (h + 1)], op0=OP.mult, op1=OP.add)
                    last.then_inc(mk.s_dve.h, 1)
                mk.s_dve.n += 1
                hs_ready = mk.s_dve.n

                @blk.tensor
                def _(t, k=k, hslot=hslot, hs_ready=hs_ready, w=pY_cons):
                    t.wait_ge(mk.s_dve.h, hs_ready)
                    wait_evt(t, w)
                    t.matmul(mk.pY[:, :], mk.C_fm[:, 3 + 128 * k: 3 + 128 * (k + 1)],
                             mk.hsbuf[:, 512 * hslot: 512 * hslot + 512],
                             start=True, stop=True).then_inc(mk.s_pe.h, 1)
                mk.s_pe.n += 1
                hs_cons[hslot] = (mk.s_pe, mk.s_pe.n)
                inter_done = mk.s_pe.n

                bank_i = (k * 8 + p) % 2

                @blk.tensor
                def _(t, k=k, p=p, rl=max(rows_ld), w1=cumbc_cons[0], w1b=cumbc_cons[1],
                      w2=(intra_bank_cons[bank_i])):
                    t.wait_ge(mk.s_rows.h, rl)
                    wait_evts(t, w1, w1b, w2)
                    last = None
                    for hh in range(8):
                        cb = mk.pb[2] if hh < 4 else mk.paux
                        rslot = (2 * p + hh // 4) % 2
                        sl = cb[:, 128 * (hh % 4): 128 * (hh % 4 + 1)]
                        t.matmul(sl, mk.ones_f[:, :],
                                 mk.rows[0:1, 512 * rslot + 128 * (hh % 4): 512 * rslot + 128 * (hh % 4 + 1)],
                                 start=True, stop=False)
                        last = t.matmul(sl, mk.pen[:, :], mk.ident_f[:, :],
                                        start=False, stop=True)
                    last.then_inc(mk.s_pe.h, 1)
                mk.s_pe.n += 1
                rows_cons[0] = rows_cons[1] = (mk.s_pe, mk.s_pe.n)

                @blk.scalar
                def _(sc, k=k, p=p, pe_need=mk.s_pe.n, w=intra_done):
                    sc.wait_ge(mk.s_pe.h, pe_need)
                    wait_evt(sc, w)
                    last = None
                    for hh in range(8):
                        h = 8 * p + hh
                        cb = mk.pb[2] if hh < 4 else mk.paux
                        last = sc.activation(out=mk.lam[:, 128 * hh: 128 * (hh + 1)],
                                             in_=cb[:, 128 * (hh % 4): 128 * (hh % 4 + 1)],
                                             func=AF.Exp, bias=mk.transT[:, NH + h: NH + h + 1])
                    last.then_inc(mk.s_act.h, 1)
                mk.s_act.n += 1
                cumbc_cons[0] = cumbc_cons[1] = (mk.s_act, mk.s_act.n)
                exp_cons = (mk.s_act, mk.s_act.n)

                @blk.vector
                def _(v, k=k):
                    v.wait_ge(mk.s_act.h, mk.s_act.n)
                    last = None
                    for hh in range(8):
                        last = v.tensor_tensor(out=mk.lam[:, 128 * hh: 128 * (hh + 1)],
                                               in0=mk.lam[:, 128 * hh: 128 * (hh + 1)],
                                               in1=mk.GT_all[:, 128 * k: 128 * (k + 1)],
                                               op=OP.mult)
                    last.then_inc(mk.s_dve.h, 1)
                mk.s_dve.n += 1

                @blk.tensor
                def _(t, k=k, p=p, bank_i=bank_i, dve_need=mk.s_dve.n):
                    t.wait_ge(mk.s_dve.h, dve_need)
                    last = None
                    for hh in range(8):
                        last = t.matmul(mk.pb[bank_i][:, 64 * hh: 64 * (hh + 1)],
                                        mk.lam[:, 128 * hh: 128 * (hh + 1)],
                                        mk.Sbf_a[:, 64 * (8 * p + hh): 64 * (8 * p + hh + 1)],
                                        start=(hh == 0), stop=(hh == 7))
                    last.then_inc(mk.s_pe.h, 1)
                mk.s_pe.n += 1
                intra_done = (mk.s_pe, mk.s_pe.n)

                @blk.vector
                def _(v, k=k, p=p, bank_i=bank_i, ecum_ready=ecum_ready):
                    v.wait_ge(mk.s_pe.h, mk.s_pe.n)
                    v.wait_ge(mk.s_act.h, ecum_ready)
                    last = None
                    for hh in range(8):
                        h = 8 * p + hh
                        sl = mk.Sbf_a[:, 64 * h: 64 * (h + 1)]
                        last = v.scalar_tensor_tensor(out=sl, in0=sl, scalar=mk.Dbc[:, h: h + 1],
                                                      in1=mk.pb[bank_i][:, 64 * hh: 64 * (hh + 1)],
                                                      op0=OP.mult, op1=OP.add)
                    v.drain()
                    for hh in range(8):
                        h = 8 * p + hh
                        sl = mk.Sbf_a[:, 64 * h: 64 * (h + 1)]
                        last = v.scalar_tensor_tensor(out=sl, in0=mk.pY[:, 64 * hh: 64 * (hh + 1)],
                                                      scalar=mk.ecumT[:, h: h + 1], in1=sl,
                                                      op0=OP.mult, op1=OP.add)
                    last.then_inc(mk.s_dve.h, 1)
                mk.s_dve.n += 1
                pY_cons = (mk.s_dve, mk.s_dve.n)
                intra_bank_cons[bank_i] = (mk.s_dve, mk.s_dve.n)
            transT_cons = (mk.s_dve, mk.s_dve.n)

            @blk.gpsimd
            def _(g, k=k, w=sbfb_free):
                wait_evt(g, w)
                g.dma_start(out=mk.Sbf_b[:, :],
                            in_=mk.z_scr[128 * k: 128 * (k + 1), :]).then_inc(mk.s_w.h, 16)
            mk.s_w.n += 16

            @blk.scalar
            def _(sc, k=k, ld=mk.s_w.n):
                sc.wait_ge(mk.s_w.h, ld)
                sc.activation(out=mk.Sbf_b[:, :], in_=mk.Sbf_b[:, :],
                              func=AF.Silu).then_inc(mk.s_act.h, 1)
            mk.s_act.n += 1

            @blk.vector
            def _(v, k=k):
                v.wait_ge(mk.s_act.h, mk.s_act.n)
                v.tensor_tensor(out=mk.Sbf_b[:, :], in0=mk.Sbf_a[:, :], in1=mk.Sbf_b[:, :],
                                op=OP.mult)
                v.drain()
                v.scalar_tensor_tensor(out=mk.Sbf_a[:, :], in0=mk.Sbf_b[:, :], scalar=1.0,
                                       in1=mk.Sbf_b[:, :], op0=OP.mult, op1=OP.mult,
                                       accum_out=mk.stat[:, 11:12])
                v.drain()
                v.tensor_scalar(out=mk.stat[:, 12:13], in0=mk.stat[:, 11:12],
                                scalar1=1.0 / 4096, scalar2=EPS, op0=OP.mult, op1=OP.add)
                v.engine_nop().then_inc(mk.s_dve.h, 1)
            mk.s_dve.n += 1
            sbfa_free = (mk.s_dve, mk.s_dve.n)

            @blk.scalar
            def _(sc, k=k):
                sc.wait_ge(mk.s_dve.h, mk.s_dve.n)
                sc.activation(out=mk.stat[:, 13:14], in_=mk.stat[:, 12:13], func=AF.Ln)
                sc.drain()
                sc.activation(out=mk.stat[:, 14:15], in_=mk.stat[:, 13:14], func=AF.Exp,
                              scale=-0.5).then_inc(mk.s_act.h, 1)
            mk.s_act.n += 1

            @blk.vector
            def _(v, k=k):
                v.wait_ge(mk.s_act.h, mk.s_act.n)
                v.tensor_scalar(out=mk.Sbf_b[:, :], in0=mk.Sbf_b[:, :],
                                scalar1=mk.stat[:, 14:15], scalar2=None,
                                op0=OP.mult, op1=OP.bypass).then_inc(mk.s_dve.h, 1)
            mk.s_dve.n += 1
            gn_ready = mk.s_dve.n

            for q in range(8):
                @blk.tensor
                def _(t, k=k, q=q, gn_ready=gn_ready, w=ptr_free[q % 2]):
                    t.wait_ge(mk.s_dve.h, gn_ready)
                    wait_evt(t, w)
                    bank = mk.ptr[q % 2]
                    last = None
                    for i in range(4):
                        c = 4 * q + i
                        last = t.transpose(bank[0:128, i * 128: (i + 1) * 128],
                                           mk.Sbf_b[:, 128 * c: 128 * (c + 1)],
                                           mk.ident_b[:, :])
                    last.then_inc(mk.s_pe.h, 1)
                mk.s_pe.n += 1

                @blk.vector
                def _(v, k=k, q=q, w=zw_cons2[q % 2]):
                    v.wait_ge(mk.s_pe.h, mk.s_pe.n)
                    wait_evt(v, w)
                    bank = mk.ptr[q % 2]
                    last = None
                    for i in range(4):
                        last = v.tensor_copy(
                            mk.Sbf_a[:, 512 * (q % 2) + 128 * i: 512 * (q % 2) + 128 * (i + 1)],
                            bank[0:128, i * 128: (i + 1) * 128])
                    last.then_inc(mk.s_dve.h, 1)
                mk.s_dve.n += 1
                ptr_free[q % 2] = (mk.s_dve, mk.s_dve.n)

                @blk.sync
                def _(sync, k=k, q=q):
                    sync.wait_ge(mk.s_dve.h, mk.s_dve.n)
                    sync.dma_start(out=mk.gnt_scr[k, q, :, :],
                                   in_=mk.Sbf_a[:, 512 * (q % 2): 512 * (q % 2) + 512]
                                   ).then_inc(mk.s_st.h, 16)
                mk.s_st.n += 16
                zw_cons2[q % 2] = (mk.s_st, mk.s_st.n)

            sbfb_free = (mk.s_pe, mk.s_pe.n)

        self.ptr_free = ptr_free
        self.intra_bank_cons = intra_bank_cons
        self.woutbf_ready = (mk.s_gst, mk.s_gst.n)
        self.cumbc_pb2_cons = cumbc_cons[0]
        self.cumbc_paux_cons = cumbc_cons[0]
        self.gnt_stored = (mk.s_st, mk.s_st.n)
        if self.debug and self.stage == 5:
            self.dbg_items += [("h_in", self.S32a), ("wbc", self.wbc),
                               ("gnt", self.gnt_scr),
                               ("zscr", self.z_scr),
                               ("xsscr", self.xs_scr)]

    def p6_experts(self):
        mk = self
        blk = self.block
        uidx = self.uidx
        bank_cons = {0: self.intra_bank_cons[0], 1: self.intra_bank_cons[1],
                     2: self.cumbc_pb2_cons}
        u_cons = [self.gnt_stored] * 4
        brow_cons = None
        acc_stored = None

        for p in range(8):
            for e in range(8):
                units = []
                for h in range(2):
                    src = mk.W_e[e, 2048 * h: 2048 * (h + 1), 512 * p: 512 * (p + 1)].rearrange(
                        "(kc p) c -> p kc c", p=128)
                    units.append(self.w_unit_load(uidx, src, 512, None))
                    uidx += 1

                @blk.gpsimd
                def _(g, p=p, e=e, w=brow_cons):
                    wait_evt(g, w)
                    g.dma_start(out=mk.brow[0:1, 0:512],
                                in_=mk.b_e[e, 512 * p: 512 * (p + 1)].rearrange("(o c) -> o c", o=1)
                                ).then_inc(mk.s_w.h, 16)
                mk.s_w.n += 16
                brow_ld = mk.s_w.n

                for t_i in range(NT):
                    bank_i = (p * 64 + e * 8 + t_i) % 3
                    bank = mk.pb[bank_i]
                    uslot = (e * 8 + t_i) % 4

                    @blk.tensor
                    def _(t, p=p, e=e, t_i=t_i, bank=bank, units=units, brow_ld=brow_ld,
                          w=bank_cons[bank_i]):
                        wait_evt(t, w)
                        t.wait_ge(mk.s_w.h, brow_ld)
                        t.matmul(bank[:, :], mk.ones_b[:, :], mk.brow[0:1, 0:512],
                                 start=True, stop=False)
                        for h in range(2):
                            wait_evt(t, units[h][1])
                            ub = units[h][0]
                            for kc in range(16):
                                last = t.matmul(bank[:, :], mk.xnT_sl(16 * h + kc, 3 + 128 * t_i, 128),
                                                mk.Wbuf[:, ub + kc * 512: ub + (kc + 1) * 512],
                                                start=False, stop=(h == 1 and kc == 15))
                        last.then_inc(mk.s_pe.h, 1)
                    mk.s_pe.n += 1
                    if t_i == NT - 1:
                        self.unit_done[uidx - 2] = (mk.s_pe, mk.s_pe.n)
                        self.unit_done[uidx - 1] = (mk.s_pe, mk.s_pe.n)
                        brow_cons = (mk.s_pe, mk.s_pe.n)

                    @blk.scalar
                    def _(sc, p=p, e=e, t_i=t_i, bank=bank, uslot=uslot, w=u_cons[uslot]):
                        sc.wait_ge(mk.s_pe.h, mk.s_pe.n)
                        wait_evt(sc, w)
                        sc.activation(out=mk.Sbf_a[:, 512 * uslot: 512 * (uslot + 1)],
                                      in_=bank[:, :], func=AF.Relu,
                                      scale=mk.masks[:, 8 * t_i + e: 8 * t_i + e + 1]
                                      ).then_inc(mk.s_act.h, 1)
                    mk.s_act.n += 1
                    bank_cons[bank_i] = (mk.s_act, mk.s_act.n)

                    @blk.vector
                    def _(v, e=e, t_i=t_i, uslot=uslot, w=acc_stored):
                        v.wait_ge(mk.s_act.h, mk.s_act.n)
                        usl = mk.Sbf_a[:, 512 * uslot: 512 * (uslot + 1)]
                        asl = mk.S32a[:, 512 * t_i: 512 * (t_i + 1)]
                        if e == 0:
                            wait_evt(v, w)
                            v.tensor_copy(asl, usl).then_inc(mk.s_dve.h, 1)
                        else:
                            v.tensor_tensor(out=asl, in0=asl, in1=usl, op=OP.add).then_inc(mk.s_dve.h, 1)
                    mk.s_dve.n += 1
                    u_cons[uslot] = (mk.s_dve, mk.s_dve.n)

            @blk.sync
            def _(sync, p=p):
                sync.wait_ge(mk.s_dve.h, mk.s_dve.n)
                sync.dma_start(out=mk.ffn_scr.rearrange("(t p) C -> p t C", p=128)[:, :, 512 * p: 512 * (p + 1)],
                               in_=mk.S32a.ap().rearrange("p (t c) -> p t c", t=8)).then_inc(mk.s_st.h, 16)
            mk.s_st.n += 16
            acc_stored = (mk.s_st, mk.s_st.n)

        self.bank_cons6 = bank_cons
        self.ffn_stored = acc_stored
        self.uidx = uidx
        self.p6_dve_done = (mk.s_dve, mk.s_dve.n)
        if self.debug and self.stage == 6:
            self.dbg_items += [("ffn", self.ffn_scr),
                               ("masks", self.masks)]

    def p7_outproj(self):
        mk = self
        blk = self.block
        uidx = self.uidx
        banks4 = [mk.pb[0], mk.pb[1], mk.pb[2], mk.paux]
        bank_cons = {0: self.bank_cons6[0], 1: self.bank_cons6[1], 2: self.bank_cons6[2],
                     3: self.cumbc_paux_cons}
        gt_cons = [self.p6_dve_done, self.p6_dve_done]
        op_cons = [None, None]
        out_cons = [None, None]

        for cp in range(8):
            for tq in range(2):
                units = []
                for h in range(2):
                    srcw = mk.woutbf[2048 * h: 2048 * (h + 1), 512 * cp: 512 * (cp + 1)].rearrange(
                        "(kc p) c -> p kc c", p=128)
                    units.append(self.w_unit_load(uidx, srcw, 512, (mk.nw_c, 16 * h),
                                                  pre=self.woutbf_ready))
                    uidx += 1
                for half in range(2):
                    for ti4 in range(4):
                        t_i = 4 * tq + ti4
                        gslot = (half * 4 + ti4) % 2

                        @blk.sync
                        def _(sync, t_i=t_i, half=half, gslot=gslot,
                              w1=gt_cons[gslot], w2=self.gnt_stored):
                            wait_evts(sync, w1, w2)
                            for q4 in range(4):
                                q = 4 * half + q4
                                gt = mk.Sbf_a if gslot == 0 else mk.Sbf_b
                                sync.dma_start(
                                    out=gt[:, 512 * q4: 512 * (q4 + 1)],
                                    in_=mk.gnt_scr[t_i, q, :, :]).then_inc(mk.s_ld.h, 16)
                        mk.s_ld.n += 64

                        @blk.tensor
                        def _(t, cp=cp, half=half, ti4=ti4, gslot=gslot, units=units,
                              w=(bank_cons[ti4] if half == 0 else None), ld=mk.s_ld.n):
                            t.wait_ge(mk.s_ld.h, ld)
                            wait_evt(t, units[half][1])
                            wait_evt(t, w)
                            last = None
                            ub = units[half][0]
                            for kc in range(16):
                                q4, i = kc // 4, kc % 4
                                gt = mk.Sbf_a if gslot == 0 else mk.Sbf_b
                                last = t.matmul(banks4[ti4][:, :],
                                                gt[:, 512 * q4 + 128 * i: 512 * q4 + 128 * (i + 1)],
                                                mk.Wbuf[:, ub + kc * 512: ub + (kc + 1) * 512],
                                                start=(half == 0 and kc == 0),
                                                stop=(half == 1 and kc == 15))
                            last.then_inc(mk.s_pe.h, 1)
                        mk.s_pe.n += 1
                        gt_cons[gslot] = (mk.s_pe, mk.s_pe.n)
                    self.unit_done[uidx - 2 + half] = (mk.s_pe, mk.s_pe.n)
                mm_done = mk.s_pe.n

                for ti4 in range(4):
                    t_i = 4 * tq + ti4
                    lo = 3 + 128 * t_i
                    oslot = ti4 % 2

                    @blk.sync
                    def _(sync, cp=cp, t_i=t_i, lo=lo, oslot=oslot,
                          w1=op_cons[oslot], w2=self.ffn_stored):
                        wait_evts(sync, w1, w2)
                        sync.dma_start(out=mk.S32b[:, 512 * oslot: 512 * (oslot + 1)],
                                       in_=mk.x_in[lo: lo + 128, 512 * cp: 512 * (cp + 1)]
                                       ).then_inc(mk.s_ld.h, 16)
                        sync.dma_start(out=mk.S32b[:, 1024 + 512 * oslot: 1536 + 512 * oslot],
                                       in_=mk.ffn_scr[128 * t_i: 128 * (t_i + 1), 512 * cp: 512 * (cp + 1)]
                                       ).then_inc(mk.s_ld.h, 16)
                    mk.s_ld.n += 32

                    @blk.vector
                    def _(v, ti4=ti4, oslot=oslot, mm_done=mm_done, w=out_cons[oslot], ld=mk.s_ld.n):
                        v.wait_ge(mk.s_ld.h, ld)
                        v.wait_ge(mk.s_pe.h, mm_done)
                        wait_evt(v, w)
                        osl = mk.S32b[:, 2048 + 512 * oslot: 2560 + 512 * oslot]
                        v.tensor_tensor(out=osl, in0=banks4[ti4][:, :],
                                        in1=mk.S32b[:, 512 * oslot: 512 * (oslot + 1)], op=OP.add)
                        v.drain()
                        v.tensor_tensor(out=osl, in0=osl,
                                        in1=mk.S32b[:, 1024 + 512 * oslot: 1536 + 512 * oslot],
                                        op=OP.add).then_inc(mk.s_dve.h, 1)
                    mk.s_dve.n += 1
                    op_cons[oslot] = (mk.s_dve, mk.s_dve.n)
                    bank_cons[ti4] = (mk.s_dve, mk.s_dve.n)

                    @blk.sync
                    def _(sync, cp=cp, t_i=t_i, oslot=oslot):
                        sync.wait_ge(mk.s_dve.h, mk.s_dve.n)
                        sync.dma_start(out=mk.out[128 * t_i: 128 * (t_i + 1), 512 * cp: 512 * (cp + 1)],
                                       in_=mk.S32b[:, 2048 + 512 * oslot: 2560 + 512 * oslot]
                                       ).then_inc(mk.s_fin.h, 16)
                    mk.s_fin.n += 16
                    out_cons[oslot] = (mk.s_fin, mk.s_fin.n)

    def finalize_program(self):
        mk = self
        blk = self.block
        outs = []
        for name, sb in self.dbg_items:
            apx = sb.ap() if callable(getattr(sb, "ap", None)) else sb
            shp = list(apx.tensor.shape)
            dt = apx.tensor.dtype
            o = self.nc.dram_tensor("dbg_" + name, shp, dt, kind="ExternalOutput").ap()
            outs.append((o, apx))

        @blk.sync
        def _(sync):
            sync.wait_ge(mk.s_dve.h, mk.s_dve.n)
            sync.wait_ge(mk.s_act.h, mk.s_act.n)
            sync.wait_ge(mk.s_pe.h, mk.s_pe.n)
            sync.wait_ge(mk.s_pool.h, mk.s_pool.n)
            sync.wait_ge(mk.s_st.h, mk.s_st.n)
            sync.wait_ge(mk.s_w.h, mk.s_w.n)
            if mk.s_rows.n:
                sync.wait_ge(mk.s_rows.h, mk.s_rows.n)
            if mk.s_gst.n:
                sync.wait_ge(mk.s_gst.h, mk.s_gst.n)
            if mk.s_cc.n:
                sync.wait_ge(mk.s_cc.h, mk.s_cc.n)
            for o, sb in outs:
                sync.dma_start(out=o, in_=sb).then_inc(mk.s_fin.h, 16)
                mk.s_fin.n += 16
            if mk.s_fin.n == 0:
                sync.dma_start(out=mk.out[0:128, :], in_=mk.S32a[:, :]).then_inc(mk.s_fin.h, 16)
                mk.s_fin.n += 16
            sync.wait_ge(mk.s_fin.h, mk.s_fin.n)


def build_nc(stage=STAGE, debug=DEBUG):
    m = MK(stage=stage, debug=debug)
    nc = m.build()
    return nc, m


def shard_inputs(inputs):
    x = np.asarray(inputs["L0_attn_hidden_states"], np.float32)
    full = {
        "Wp": np.ascontiguousarray(np.asarray(inputs["L0_attn_in_proj_w"], np.float32)),
        "conv_w": np.ascontiguousarray(np.asarray(inputs["L0_attn_conv_w"], np.float32)),
        "A_log": np.ascontiguousarray(np.asarray(inputs["L0_attn_A_log"], np.float32)),
        "dt_b": np.ascontiguousarray(np.asarray(inputs["L0_attn_dt_bias"], np.float32)),
        "D_in": np.ascontiguousarray(np.asarray(inputs["L0_attn_D"], np.float32)),
        "ln_w": np.ascontiguousarray(np.asarray(inputs["L0_attn_ln_w"], np.float32)),
        "norm_w": np.ascontiguousarray(np.asarray(inputs["L0_attn_norm_w"], np.float32)),
        "Wout": np.ascontiguousarray(np.asarray(inputs["L0_attn_out_proj_w"], np.float32)),
        "mup": np.ascontiguousarray(np.asarray(inputs["L0_attn_mup_vector"], np.float32)),
        "Wg": np.ascontiguousarray(np.asarray(inputs["L0_ffn_W_gate"], np.float32)),
        "bg": np.ascontiguousarray(np.asarray(inputs["L0_ffn_b_gate"], np.float32)),
        "W_e": np.ascontiguousarray(np.asarray(inputs["L0_ffn_W_e"], np.float32)),
        "b_e": np.ascontiguousarray(np.asarray(inputs["L0_ffn_b_e"], np.float32)),
    }
    in_maps = []
    for c in range(8):
        b, q = c // 4, c % 4
        lo = 1024 * q - 3
        if lo < 0:
            xc = np.concatenate([np.zeros((3, 4096), np.float32), x[b, 0:1024]], 0)
        else:
            xc = np.ascontiguousarray(x[b, lo:1024 * q + 1024])
        selv = np.zeros((64, 4), np.float32)
        mulv = np.zeros((64, 4), np.float32)
        selv[:, :q] = 1.0
        mulv[:, :q] = 1.0
        m = dict(full)
        m["x_in"] = xc
        m["selv"] = selv
        m["mulv"] = mulv
        in_maps.append(m)
    return in_maps


def kernel(**inputs):
    in_maps = shard_inputs(inputs)
    nc, _ = build_nc(stage=7, debug=False)
    res = run_bass_kernel_spmd(nc, in_maps, core_ids=list(range(8)))
    outs = [res.results[c]["out"] for c in range(8)]
    return np.stack([np.concatenate(outs[4 * b: 4 * b + 4], 0) for b in range(2)], 0)


# version 06
# speedup vs baseline: 1.0918x; 1.0918x over previous
import numpy as np
import concourse.bass as bass
import concourse.mybir as mybir
from concourse.bass_utils import run_bass_kernel_spmd
from concourse.alu_op_type import AluOpType
from contextlib import ExitStack

F32 = mybir.dt.float32
BF16 = mybir.dt.bfloat16
AF = mybir.ActivationFunctionType
OP = AluOpType

EPS = 1e-5
NTOK = 1024
NLOC = 1027
NT = 8
NC_D = 32
T = 128
NH = 64
S = 128
D_MODEL = 4096
PROJ = 8512
CONV = 4352
NCC = 34
XNW = 1028

STAGE = 7
DEBUG = False


class Sem:
    def __init__(self, nc, es, name):
        self.h = es.enter_context(nc.semaphore(name))
        self.n = 0

    def now(self):
        return (self, self.n)


def wait_evt(eng, evt):
    if evt is not None:
        sem, n = evt
        if n > 0:
            eng.wait_ge(sem.h, n)


def wait_evts(eng, *evts):
    best = {}
    for e in evts:
        if e is None:
            continue
        sem, n = e
        if n > 0 and (sem not in best or best[sem] < n):
            best[sem] = n
    for sem, n in best.items():
        eng.wait_ge(sem.h, n)


class MK:
    def __init__(self, stage=7, debug=False):
        self.stage = stage
        self.debug = debug
        nc = bass.Bass("TRN2", target_bir_lowering=False)
        self.nc = nc
        self.es = ExitStack()
        self.dbg_items = []
        self._mk_dram()
        self._mk_sbuf()
        self._mk_psum()
        self._mk_sems()
        self.block = self.es.enter_context(nc.Block())
        self.unit_done = {}

    def _mk_dram(self):
        nc = self.nc
        D = lambda n, s, d, k=None: (nc.dram_tensor(n, s, d, kind=k).ap() if k
                                     else nc.dram_tensor(n, s, d).ap())
        self.x_in = D("x_in", [NLOC, D_MODEL], F32, "ExternalInput")
        self.Wp = D("Wp", [D_MODEL, PROJ], F32, "ExternalInput")
        self.conv_w = D("conv_w", [CONV, 4], F32, "ExternalInput")
        self.A_log = D("A_log", [NH], F32, "ExternalInput")
        self.dt_b = D("dt_b", [NH], F32, "ExternalInput")
        self.D_in = D("D_in", [NH], F32, "ExternalInput")
        self.ln_w = D("ln_w", [D_MODEL], F32, "ExternalInput")
        self.norm_w = D("norm_w", [D_MODEL], F32, "ExternalInput")
        self.Wout = D("Wout", [D_MODEL, D_MODEL], F32, "ExternalInput")
        self.mup = D("mup", [PROJ], F32, "ExternalInput")
        self.Wg = D("Wg", [D_MODEL, 8], F32, "ExternalInput")
        self.bg = D("bg", [8], F32, "ExternalInput")
        self.W_e = D("W_e", [8, D_MODEL, D_MODEL], F32, "ExternalInput")
        self.b_e = D("b_e", [8, D_MODEL], F32, "ExternalInput")
        self.selv = D("selv", [NH, 4], F32, "ExternalInput")
        self.mulv = D("mulv", [NH, 4], F32, "ExternalInput")
        self.out = D("out", [NTOK, D_MODEL], F32, "ExternalOutput")
        self.z_scr = D("z_scr", [NTOK, D_MODEL], BF16)
        self.xs_scr = D("xs_scr", [NTOK, D_MODEL], BF16)
        self.sloc_scr = D("sloc_scr", [7, S, D_MODEL], F32)
        self.gnt_scr = D("gnt_scr", [NT, 8, 128, 512], BF16)
        self.ffn_scr = D("ffn_scr", [NTOK, D_MODEL], F32)
        self.cum_scr = D("cum_scr", [NH, NTOK], F32)
        self.woutbf = D("woutbf", [D_MODEL, D_MODEL], BF16)
        self.ag_in = D("ag_in", [S, D_MODEL], BF16)
        self.ag_out = D("ag_out", [4 * S, D_MODEL], BF16)
        self.agp_in = D("agp_in", [NH, 1], F32)
        self.agp_out = D("agp_out", [4 * NH, 1], F32)

    def _mk_sbuf(self):
        nc, es = self.nc, self.es
        SB = lambda n, s, d: es.enter_context(nc.sbuf_tensor(n, s, d))
        self.xnT = SB("xnT", [128, NC_D * XNW], BF16)
        self.Wbuf = SB("Wbuf", [128, 3 * 8192], BF16)
        self.S32a = SB("S32a", [128, 4096], F32)
        self.S32b = SB("S32b", [128, 4096], F32)
        self.Sbf_a = SB("Sbf_a", [128, 4096], BF16)
        self.Sbf_b = SB("Sbf_b", [128, 4096], BF16)
        self.scr32 = SB("scr32", [128, 1024], F32)
        self.B_fm = SB("B_fm", [128, XNW], BF16)
        self.C_fm = SB("C_fm", [128, XNW], BF16)
        self.B_tok = SB("B_tok", [128, 1024], BF16)
        self.GT_all = SB("GT_all", [128, 1024], BF16)
        self.dt_sb = SB("dt_sb", [NH, 1024], F32)
        self.cum = SB("cum", [NH, 1024], F32)
        self.cs2 = SB("cs2", [NH, 1024], F32)
        self.zeros64 = SB("zeros64", [NH, T], F32)
        self.transT = SB("transT", [128, 128], F32)
        self.ecumT = SB("ecumT", [128, NH], F32)
        self.wq = SB("wq", [NH, T], F32)
        self.wqT = SB("wqT", [128, NH], F32)
        self.aTbc = SB("aTbc", [128, NH], F32)
        self.Qbc = SB("Qbc", [128, 8 * NH], F32)
        self.Pcol = SB("Pcol", [NH, 2], F32)
        self.hsbuf = SB("hsbuf", [128, 2 * 512], BF16)
        self.wbc = SB("wbc", [128, 4 * NH], F32)
        self.gsm = SB("gsm", [NH, 16], F32)
        self.rows = SB("rows", [1, 1024], F32)
        self.lam = SB("lam", [128, 1024], BF16)
        self.mup_fm = SB("mup_fm", [128, NCC + 2], F32)
        self.mup_dt = SB("mup_dt", [NH, 1], F32)
        self.r2bc = SB("r2bc", [128, XNW], F32)
        self.r2col = SB("r2col", [128, 9], F32)
        self.stat = SB("stat", [128, 16], F32)
        self.lnw_c = SB("lnw_c", [128, NC_D], F32)
        self.nw_c = SB("nw_c", [128, NC_D], F32)
        self.Acol = SB("Acol", [NH, 1], F32)
        self.dtbcol = SB("dtbcol", [NH, 1], F32)
        self.Dbc = SB("Dbc", [128, NH], F32)
        self.convw = SB("convw", [128, NCC * 4], F32)
        self.ident_f = SB("ident_f", [128, 128], F32)
        self.ident_b = SB("ident_b", [128, 128], BF16)
        self.pen = SB("pen", [128, 128], BF16)
        self.ones_f = SB("ones_f", [1, 128], F32)
        self.ones_b = SB("ones_b", [1, 128], BF16)
        self.masks = SB("masks", [128, NT * 8], F32)
        self.mx = SB("mx", [128, 2], F32)
        self.Wg_sb = SB("Wg_sb", [128, NC_D * 8], BF16)
        self.row1 = SB("row1", [1, 128], F32)
        self.brow = SB("brow", [1, 520], BF16)

    def _mk_psum(self):
        nc, es = self.nc, self.es
        P = lambda n, s, d: es.enter_context(nc.psum_tensor(n, s, d))
        self.pb = [P(f"pb{i}", [128, 512], F32) for i in range(3)]
        self.paux = P("paux", [128, 512], F32)
        self.pscan = P("pscan", [128, 512], F32)
        self.pY = P("pY", [128, 512], F32)
        self.ptrA = P("ptrA", [128, 1024], BF16)
        self.ptrB = P("ptrB", [128, 1024], BF16)
        self.ptr = [self.ptrA, self.ptrB]

    def _mk_sems(self):
        nc, es = self.nc, self.es
        mk = lambda n: Sem(nc, es, n)
        self.s_pre = mk("s_pre")
        self.s_ld = mk("s_ld")
        self.s_w = mk("s_w")
        self.s_wsc = mk("s_wsc")
        self.s_st = mk("s_st")
        self.s_gst = mk("s_gst")
        self.s_pe = mk("s_pe")
        self.s_dve = mk("s_dve")
        self.s_act = mk("s_act")
        self.s_pool = mk("s_pool")
        self.s_cc = mk("s_cc")
        self.s_rows = mk("s_rows")
        self.s_fin = mk("s_fin")

    def xnT_sl(self, kc, lo, n):
        return self.xnT[:, kc * XNW + lo: kc * XNW + lo + n]

    def build(self):
        self.p0_preamble()
        self.p1_layernorm()
        if self.stage >= 2:
            self.p2_inproj()
        if self.stage >= 3:
            self.p3_dtchain()
        if self.stage >= 4:
            self.p4_passA()
        if self.stage >= 5:
            self.p5_passB()
        if self.stage >= 6:
            self.p6_experts()
        if self.stage >= 7:
            self.p7_outproj()
        self.finalize_program()
        self.es.close()
        return self.nc

    def p0_preamble(self):
        mk = self
        blk = self.block

        @blk.gpsimd
        def _(g):
            for (dst, src) in [
                (mk.scr32[0:NC_D, 0:128], mk.ln_w.rearrange("(c p) -> c p", p=128)),
                (mk.scr32[0:NC_D, 128:256], mk.norm_w.rearrange("(c p) -> c p", p=128)),
                (mk.Acol[:, :], mk.A_log.rearrange("(h o) -> h o", o=1)),
                (mk.dtbcol[:, :], mk.dt_b.rearrange("(h o) -> h o", o=1)),
                (mk.convw.ap().rearrange("p (c k) -> p c k", k=4),
                 mk.conv_w.rearrange("(c p) k -> p c k", p=128)),
                (mk.scr32[0:NCC, 256:384], mk.mup[4096:8448].rearrange("(c p) -> c p", p=128)),
                (mk.mup_dt[:, :], mk.mup[8448:8512].rearrange("(h o) -> h o", o=1)),
                (mk.Wg_sb.ap().rearrange("p (c e) -> p c e", e=8),
                 mk.Wg.rearrange("(c p) e -> p c e", p=128)),
                (mk.row1[0:1, 0:NH], mk.D_in.rearrange("(o h) -> o h", o=1)),
                (mk.brow[0:1, 512:520], mk.bg.rearrange("(o e) -> o e", o=1)),
                (mk.gsm[:, 0:4], mk.selv[:, :]),
                (mk.gsm[:, 4:8], mk.mulv[:, :]),
            ]:
                g.dma_start(out=dst, in_=src).then_inc(mk.s_pre.h, 16)
                mk.s_pre.n += 16
            g.memset(mk.zeros64[:, :], 0.0)
            g.memset(mk.ones_f[:, :], 1.0)
            g.memset(mk.ones_b[:, :], 1.0)
            g.memset(mk.Qbc[:, 0:NH], 1.0)
            g.memset(mk.Pcol[:, 0:1], 1.0)
            g.memset(mk.mup_fm[:, NCC:NCC + 2], 0.0)
            g.iota(mk.ident_f[:, :], pattern=[[1, 128]], base=0, channel_multiplier=-1,
                   allow_small_or_imprecise_dtypes=True)
            g.iota(mk.pen[:, :], pattern=[[1, 128]], base=0, channel_multiplier=-1,
                   allow_small_or_imprecise_dtypes=True)
            g.engine_nop().then_inc(mk.s_pre.h, 1)
            mk.s_pre.n += 1

        @blk.vector
        def _(v):
            v.wait_ge(mk.s_pre.h, mk.s_pre.n)
            v.tensor_scalar(out=mk.ident_f[:, :], in0=mk.ident_f[:, :], scalar1=0.0,
                            scalar2=None, op0=OP.is_equal, op1=OP.bypass)
            v.tensor_scalar(out=mk.pen[:, :], in0=mk.pen[:, :], scalar1=0.0, scalar2=None,
                            op0=OP.is_ge, op1=OP.bypass)
            v.drain()
            v.tensor_copy(mk.ident_b[:, :], mk.ident_f[:, :]).then_inc(mk.s_dve.h, 1)
        mk.s_dve.n += 1
        ident_ready = mk.s_dve.now()

        @blk.scalar
        def _(sc):
            sc.wait_ge(mk.s_pre.h, mk.s_pre.n)
            sc.activation(out=mk.Acol[:, :], in_=mk.Acol[:, :], func=AF.Exp).then_inc(mk.s_act.h, 1)
        mk.s_act.n += 1

        @blk.vector
        def _(v):
            v.wait_ge(mk.s_act.h, mk.s_act.n)
            v.tensor_scalar(out=mk.Acol[:, :], in0=mk.Acol[:, :], scalar1=-1.0,
                            scalar2=None, op0=OP.mult, op1=OP.bypass).then_inc(mk.s_dve.h, 1)
        mk.s_dve.n += 1

        @blk.tensor
        def _(t):
            t.wait_ge(mk.s_pre.h, mk.s_pre.n)
            wait_evt(t, ident_ready)
            t.matmul(mk.pscan[:, 0:NH], mk.ones_f[:, :], mk.row1[0:1, 0:NH],
                     start=True, stop=True)
            t.transpose(mk.pscan[0:128, NH:NH + NC_D], mk.scr32[0:NC_D, 0:128],
                        mk.ident_f[0:NC_D, 0:NC_D])
            t.transpose(mk.pscan[0:128, 128:128 + NC_D], mk.scr32[0:NC_D, 128:256],
                        mk.ident_f[0:NC_D, 0:NC_D])
            t.transpose(mk.pscan[0:128, 192:192 + NCC], mk.scr32[0:NCC, 256:384],
                        mk.ident_f[0:NCC, 0:NCC]).then_inc(mk.s_pe.h, 1)
        mk.s_pe.n += 1

        @blk.vector
        def _(v):
            v.wait_ge(mk.s_pe.h, mk.s_pe.n)
            v.tensor_copy(mk.Dbc[:, :], mk.pscan[:, 0:NH])
            v.tensor_copy(mk.lnw_c[:, :], mk.pscan[0:128, NH:NH + NC_D])
            v.tensor_copy(mk.nw_c[:, :], mk.pscan[0:128, 128:128 + NC_D])
            v.tensor_copy(mk.mup_fm[:, 0:NCC], mk.pscan[0:128, 192:192 + NCC]).then_inc(mk.s_dve.h, 1)
        mk.s_dve.n += 1
        self.pscan_free = mk.s_dve.now()

    def p1_layernorm(self):
        mk = self
        blk = self.block
        tiles = [(0, 0, 3)] + [(1 + t, 3 + 128 * t, 128) for t in range(NT)]
        xbuf_free = {0: None, 1: None}
        xn_free = None
        ptr_free = [None, None]

        for (ti, lo, n) in tiles:
            bi = ti % 2
            xt = mk.S32a if bi == 0 else mk.S32b

            @blk.sync
            def _(sync, lo=lo, n=n, xt=xt, w=xbuf_free[bi]):
                wait_evt(sync, w)
                sync.dma_start(out=xt[0:n, :], in_=mk.x_in[lo:lo + n, :]).then_inc(mk.s_ld.h, 16)
            mk.s_ld.n += 16

            @blk.vector
            def _(v, n=n, xt=xt, w=xn_free):
                v.wait_ge(mk.s_ld.h, mk.s_ld.n)
                wait_evt(v, w)
                v.tensor_scalar(out=mk.Sbf_a[0:n, :], in0=xt[0:n, :], scalar1=0.0,
                                scalar2=None, op0=OP.add, op1=OP.add,
                                accum_out=mk.stat[0:n, 0:1])
                v.drain()
                v.scalar_tensor_tensor(out=mk.Sbf_a[0:n, :], in0=xt[0:n, :], scalar=1.0,
                                       in1=xt[0:n, :], op0=OP.mult, op1=OP.mult,
                                       accum_out=mk.stat[0:n, 1:2])
                v.drain()
                v.tensor_scalar(out=mk.stat[0:n, 2:3], in0=mk.stat[0:n, 0:1],
                                scalar1=1.0 / 4096, scalar2=None, op0=OP.mult, op1=OP.bypass)
                v.drain()
                v.tensor_tensor(out=mk.stat[0:n, 3:4], in0=mk.stat[0:n, 2:3],
                                in1=mk.stat[0:n, 2:3], op=OP.mult)
                v.drain()
                v.tensor_scalar(out=mk.stat[0:n, 4:5], in0=mk.stat[0:n, 1:2],
                                scalar1=1.0 / 4096, scalar2=mk.stat[0:n, 3:4],
                                op0=OP.mult, op1=OP.subtract)
                v.drain()
                v.tensor_scalar(out=mk.stat[0:n, 5:6], in0=mk.stat[0:n, 4:5], scalar1=EPS,
                                scalar2=None, op0=OP.add, op1=OP.bypass)
                v.drain()
                v.reciprocal(mk.stat[0:n, 6:7], mk.stat[0:n, 5:6])
                v.drain()
                v.tensor_tensor(out=mk.stat[0:n, 7:8], in0=mk.stat[0:n, 4:5],
                                in1=mk.stat[0:n, 6:7], op=OP.mult)
                v.drain()
                v.tensor_scalar(out=mk.stat[0:n, 8:9], in0=mk.stat[0:n, 7:8], scalar1=EPS,
                                scalar2=None, op0=OP.add, op1=OP.bypass)
                v.drain()
                v.reciprocal(mk.stat[0:n, 9:10], mk.stat[0:n, 8:9])
                v.engine_nop().then_inc(mk.s_dve.h, 1)
            mk.s_dve.n += 1

            @blk.scalar
            def _(sc, ti=ti, n=n):
                sc.wait_ge(mk.s_dve.h, mk.s_dve.n)
                sc.activation(out=mk.stat[0:n, 10:11], in_=mk.stat[0:n, 6:7], func=AF.Sqrt)
                sc.activation(out=mk.r2col[0:n, ti:ti + 1], in_=mk.stat[0:n, 9:10],
                              func=AF.Sqrt).then_inc(mk.s_act.h, 1)
            mk.s_act.n += 1

            @blk.vector
            def _(v, n=n, xt=xt):
                v.wait_ge(mk.s_act.h, mk.s_act.n)
                v.tensor_scalar(out=mk.Sbf_a[0:n, :], in0=xt[0:n, :],
                                scalar1=mk.stat[0:n, 2:3], scalar2=mk.stat[0:n, 10:11],
                                op0=OP.subtract, op1=OP.mult)
                v.engine_nop().then_inc(mk.s_dve.h, 1)
            mk.s_dve.n += 1
            xbuf_free[bi] = mk.s_dve.now()
            xn_ready = mk.s_dve.n

            evac_evts = []

            @blk.tensor
            def _(t, ti=ti, n=n, xn_ready=xn_ready, pf=list(ptr_free)):
                t.wait_ge(mk.s_dve.h, xn_ready)
                for batch in range(4):
                    bank = mk.ptr[batch % 2]
                    w = pf[batch % 2] if batch < 2 else (mk.s_dve, xn_ready + (batch - 1))
                    wait_evt(t, w)
                    last = None
                    for j in range(8):
                        c = batch * 8 + j
                        last = t.transpose(bank[0:128, j * 128: j * 128 + n],
                                           mk.Sbf_a[0:n, c * 128:(c + 1) * 128],
                                           mk.ident_b[0:n, 0:n])
                    last.then_inc(mk.s_pe.h, 1)
                wait_evt(t, self.pscan_free)
                t.transpose(mk.pscan[0:1, 128:128 + n], mk.r2col[0:n, ti:ti + 1],
                            mk.ident_f[0:n, 0:n]).then_inc(mk.s_pe.h, 1)
            pe_pre = mk.s_pe.n
            mk.s_pe.n += 5
            xn_free = (mk.s_pe, pe_pre + 4)

            @blk.vector
            def _(v, lo=lo, n=n, pe_pre=pe_pre):
                for batch in range(4):
                    bank = mk.ptr[batch % 2]
                    v.wait_ge(mk.s_pe.h, pe_pre + batch + 1)
                    last = None
                    for j in range(8):
                        c = batch * 8 + j
                        last = v.tensor_copy(mk.xnT[:, c * XNW + lo: c * XNW + lo + n],
                                             bank[0:128, j * 128: j * 128 + n])
                    last.then_inc(mk.s_dve.h, 1)
                v.wait_ge(mk.s_pe.h, pe_pre + 5)
                v.tensor_copy(mk.row1[0:1, 0:n], mk.pscan[0:1, 128:128 + n]).then_inc(mk.s_dve.h, 1)
            mk.s_dve.n += 5
            ptr_free[0] = (mk.s_dve, mk.s_dve.n - 2)
            ptr_free[1] = (mk.s_dve, mk.s_dve.n - 1)

            @blk.tensor
            def _(t, n=n):
                t.wait_ge(mk.s_dve.h, mk.s_dve.n)
                t.matmul(mk.pscan[:, 256:256 + n], mk.ones_f[:, :], mk.row1[0:1, 0:n],
                         start=True, stop=True).then_inc(mk.s_pe.h, 1)
            mk.s_pe.n += 1

            @blk.vector
            def _(v, lo=lo, n=n):
                v.wait_ge(mk.s_pe.h, mk.s_pe.n)
                v.tensor_copy(mk.r2bc[:, lo:lo + n], mk.pscan[:, 256:256 + n]).then_inc(mk.s_dve.h, 1)
            mk.s_dve.n += 1
            self.pscan_free = mk.s_dve.now()

        self.ln_done_dve = mk.s_dve.now()
        self.ptr_free = ptr_free
        if self.debug and self.stage == 1:
            self.dbg_items += [("xnT", self.xnT), ("r2bc", self.r2bc), ("r2col", self.r2col)]

    def w_unit_load(self, uidx, src_ap, ncols, scale_cols=None, pre=None):
        mk = self
        blk = self.block
        slot = uidx % 3
        base = slot * 8192
        w_free = self.unit_done.get(uidx - 3)

        @blk.gpsimd
        def _(g):
            wait_evt(g, w_free)
            wait_evt(g, pre)
            g.dma_start(out=mk.Wbuf[:, base:base + 16 * ncols].rearrange("p (kc c) -> p kc c", c=ncols),
                        in_=src_ap).then_inc(mk.s_w.h, 16)
        mk.s_w.n += 16
        ready = (mk.s_w, mk.s_w.n)

        if scale_cols is not None:
            col_t, base_chunk = scale_cols

            @blk.vector
            def _(v, ready=ready):
                wait_evt(v, ready)
                last = None
                for kc in range(16):
                    last = v.tensor_scalar(
                        out=mk.Wbuf[:, base + kc * ncols: base + (kc + 1) * ncols],
                        in0=mk.Wbuf[:, base + kc * ncols: base + (kc + 1) * ncols],
                        scalar1=col_t[:, base_chunk + kc: base_chunk + kc + 1],
                        scalar2=None, op0=OP.mult, op1=OP.bypass)
                last.then_inc(mk.s_wsc.h, 1)
            mk.s_wsc.n += 1
            ready = (mk.s_wsc, mk.s_wsc.n)
        return base, ready

    def p2_inproj(self):
        mk = self
        blk = self.block
        self.uidx = 0
        uidx = 0
        bank_free = {i: self.ln_done_dve for i in range(3)}
        paux_free = None
        mupbc_free = None
        zw_cons = []
        ptr_free = list(self.ptr_free)
        ubuf_free = None
        xbc_slot_free = [None] * 2
        convacc_free = None
        xsfm_free = [None, None]

        for p in range(8):
            units = []
            for h in range(2):
                src = mk.Wp[2048 * h: 2048 * (h + 1), 512 * p: 512 * (p + 1)].rearrange(
                    "(kc p) c -> p kc c", p=128)
                units.append(self.w_unit_load(uidx, src, 512, (mk.lnw_c, 16 * h)))
                uidx += 1

            @blk.gpsimd
            def _(g, p=p, w=mupbc_free):
                wait_evt(g, w)
                g.dma_start(out=mk.brow[0:1, 0:512],
                            in_=mk.mup[512 * p: 512 * (p + 1)].rearrange("(o c) -> o c", o=1)
                            ).then_inc(mk.s_w.h, 16)
            mk.s_w.n += 16

            @blk.tensor
            def _(t, w=self.pscan_free, wn=mk.s_w.n):
                t.wait_ge(mk.s_w.h, wn)
                wait_evt(t, w)
                t.matmul(mk.pscan[:, 0:512], mk.ones_b[:, :], mk.brow[0:1, 0:512],
                         start=True, stop=True).then_inc(mk.s_pe.h, 1)
            mk.s_pe.n += 1

            @blk.vector
            def _(v, w=mupbc_free):
                v.wait_ge(mk.s_pe.h, mk.s_pe.n)
                wait_evt(v, w)
                v.tensor_copy(mk.hsbuf[:, 0:512], mk.pscan[:, 0:512]).then_inc(mk.s_dve.h, 1)
            mk.s_dve.n += 1
            self.pscan_free = mk.s_dve.now()
            mupbc_ready = mk.s_dve.now()

            for t_i in range(NT):
                bank_i = (p * NT + t_i) % 3
                bank = mk.pb[bank_i]

                @blk.tensor
                def _(t, p=p, t_i=t_i, bank=bank, units=units,
                      bf=bank_free[bank_i], mr=mupbc_ready):
                    wait_evts(t, bf, mr)
                    last = None
                    for h in range(2):
                        wait_evt(t, units[h][1])
                        ub = units[h][0]
                        for kc in range(16):
                            last = t.matmul(bank[:, :],
                                            mk.xnT_sl(16 * h + kc, 3 + 128 * t_i, 128),
                                            mk.Wbuf[:, ub + kc * 512: ub + (kc + 1) * 512],
                                            start=(h == 0 and kc == 0), stop=(h == 1 and kc == 15))
                    last.then_inc(mk.s_pe.h, 1)
                mk.s_pe.n += 1

                if p == 0:
                    @blk.tensor
                    def _(t, t_i=t_i, units=units, w=paux_free):
                        wait_evt(t, w)
                        for h in range(2):
                            for kc in range(16):
                                t.matmul(mk.paux[:, 8:16],
                                         mk.xnT_sl(16 * h + kc, 3 + 128 * t_i, 128),
                                         mk.Wg_sb[:, (16 * h + kc) * 8: (16 * h + kc + 1) * 8],
                                         start=(h == 0 and kc == 0), stop=False)
                        t.matmul(mk.paux[:, 8:16], mk.ones_b[:, :], mk.brow[0:1, 512:520],
                                 start=False, stop=True).then_inc(mk.s_pe.h, 1)
                    mk.s_pe.n += 1

                    @blk.vector
                    def _(v, t_i=t_i):
                        v.wait_ge(mk.s_pe.h, mk.s_pe.n)
                        msl = mk.masks[:, 8 * t_i: 8 * t_i + 8]
                        v.tensor_reduce(out=mk.mx[:, 0:1], in_=mk.paux[:, 8:16],
                                        axis=mybir.AxisListType.X, op=OP.max)
                        v.drain()
                        v.tensor_scalar(out=msl, in0=mk.paux[:, 8:16], scalar1=mk.mx[:, 0:1],
                                        scalar2=-1e30, op0=OP.is_equal, op1=OP.mult)
                        v.drain()
                        v.tensor_tensor(out=msl, in0=msl, in1=mk.paux[:, 8:16], op=OP.add)
                        v.drain()
                        v.tensor_reduce(out=mk.mx[:, 1:2], in_=msl,
                                        axis=mybir.AxisListType.X, op=OP.max)
                        v.drain()
                        v.tensor_scalar(out=msl, in0=mk.paux[:, 8:16], scalar1=mk.mx[:, 1:2],
                                        scalar2=0.5, op0=OP.is_ge, op1=OP.mult).then_inc(mk.s_dve.h, 1)
                    mk.s_dve.n += 1
                    paux_free = mk.s_dve.now()

                zidx = p * NT + t_i

                @blk.vector
                def _(v, p=p, t_i=t_i, bank=bank,
                      w=(zw_cons[zidx - 2] if zidx >= 2 else None)):
                    v.wait_ge(mk.s_pe.h, mk.s_pe.n)
                    wait_evt(v, w)
                    v.scalar_tensor_tensor(out=mk.Sbf_b[:, 2560 + 512 * (zidx % 2): 3072 + 512 * (zidx % 2)],
                                           in0=bank[:, :], scalar=mk.r2col[:, 1 + t_i: 2 + t_i],
                                           in1=mk.hsbuf[:, 0:512], op0=OP.mult,
                                           op1=OP.mult).then_inc(mk.s_dve.h, 1)
                mk.s_dve.n += 1
                bank_free[bank_i] = mk.s_dve.now()
                mupbc_free = mk.s_dve.now()

                @blk.sync
                def _(sync, p=p, t_i=t_i, zidx=zidx):
                    sync.wait_ge(mk.s_dve.h, mk.s_dve.n)
                    sync.dma_start(out=mk.z_scr[128 * t_i: 128 * (t_i + 1), 512 * p: 512 * (p + 1)],
                                   in_=mk.Sbf_b[:, 2560 + 512 * (zidx % 2): 3072 + 512 * (zidx % 2)]
                                   ).then_inc(mk.s_st.h, 16)
                mk.s_st.n += 16
                zw_cons.append((mk.s_st, mk.s_st.n))

            self.unit_done[uidx - 2] = (mk.s_pe, mk.s_pe.n)
            self.unit_done[uidx - 1] = (mk.s_pe, mk.s_pe.n)

        ccount = 0
        for p in range(9):
            ncols = 512 if p < 8 else 320
            c0 = 4096 + 512 * p
            units = []
            for h in range(2):
                src = mk.Wp[2048 * h: 2048 * (h + 1), c0: c0 + ncols].rearrange(
                    "(kc p) c -> p kc c", p=128)
                units.append(self.w_unit_load(uidx, src, ncols, (mk.lnw_c, 16 * h)))
                uidx += 1
            nchunk = 4 if p < 8 else 2
            for j in range(nchunk):
                cg = 4 * p + j
                bank_a = ccount % 3
                bank_b = (ccount + 1) % 3
                ccount += 2

                @blk.tensor
                def _(t, j=j, cg=cg, units=units, ncols=ncols, bank_a=bank_a, bank_b=bank_b,
                      bfa=bank_free[bank_a], bfb=bank_free[bank_b], pxf=paux_free):
                    wait_evts(t, bfa, bfb, pxf)
                    last = None
                    for h in range(2):
                        wait_evt(t, units[h][1])
                        ub = units[h][0]
                        for kc in range(16):
                            w_sl = mk.Wbuf[:, ub + kc * ncols + 128 * j: ub + kc * ncols + 128 * (j + 1)]
                            st = (h == 0 and kc == 0)
                            sp = (h == 1 and kc == 15)
                            t.matmul(mk.pb[bank_a][:, :], w_sl, mk.xnT_sl(16 * h + kc, 0, 512),
                                     start=st, stop=sp)
                            t.matmul(mk.pb[bank_b][:, :], w_sl, mk.xnT_sl(16 * h + kc, 512, 512),
                                     start=st, stop=sp)
                            last = t.matmul(mk.paux[:, 0:3], w_sl,
                                            mk.xnT_sl(16 * h + kc, 1024, 3),
                                            start=st, stop=sp)
                    last.then_inc(mk.s_pe.h, 1)
                mk.s_pe.n += 1

                @blk.vector
                def _(v, cg=cg, bank_a=bank_a, bank_b=bank_b, w=xbc_slot_free[cg % 2]):
                    v.wait_ge(mk.s_pe.h, mk.s_pe.n)
                    wait_evt(v, w)
                    sl = mk.Sbf_b[:, XNW * (cg % 2): XNW * (cg % 2) + XNW]
                    mcol = mk.mup_fm[:, cg: cg + 1]
                    v.scalar_tensor_tensor(out=sl[:, 0:512], in0=mk.pb[bank_a][:, :],
                                           scalar=mcol, in1=mk.r2bc[:, 0:512],
                                           op0=OP.mult, op1=OP.mult)
                    v.scalar_tensor_tensor(out=sl[:, 512:1024], in0=mk.pb[bank_b][:, :],
                                           scalar=mcol, in1=mk.r2bc[:, 512:1024],
                                           op0=OP.mult, op1=OP.mult)
                    v.scalar_tensor_tensor(out=sl[:, 1024:1027], in0=mk.paux[:, 0:3],
                                           scalar=mcol, in1=mk.r2bc[:, 1024:1027],
                                           op0=OP.mult, op1=OP.mult).then_inc(mk.s_dve.h, 1)
                mk.s_dve.n += 1
                bank_free[bank_a] = mk.s_dve.now()
                bank_free[bank_b] = mk.s_dve.now()
                paux_free = mk.s_dve.now()

                @blk.vector
                def _(v, cg=cg, w=convacc_free):
                    wait_evt(v, w)
                    sl = mk.Sbf_b[:, XNW * (cg % 2): XNW * (cg % 2) + XNW]
                    v.tensor_scalar(out=mk.scr32[:, :], in0=sl[:, 0:1024],
                                    scalar1=mk.convw[:, 4 * cg: 4 * cg + 1], scalar2=None,
                                    op0=OP.mult, op1=OP.bypass)
                    for k in range(1, 4):
                        v.drain()
                        v.scalar_tensor_tensor(out=mk.scr32[:, :], in0=sl[:, k:k + 1024],
                                               scalar=mk.convw[:, 4 * cg + k: 4 * cg + k + 1],
                                               in1=mk.scr32[:, :], op0=OP.mult, op1=OP.add)
                    v.engine_nop().then_inc(mk.s_dve.h, 1)
                mk.s_dve.n += 1
                xbc_slot_free[cg % 2] = (mk.s_dve, mk.s_dve.n)
                conv_dve = mk.s_dve.n

                @blk.scalar
                def _(sc, cg=cg, dn=conv_dve, w=xsfm_free[cg % 2] if cg < 32 else None):
                    sc.wait_ge(mk.s_dve.h, dn)
                    wait_evt(sc, w)
                    if cg < 32:
                        dst = mk.Sbf_a[:, 1024 * (cg % 2): 1024 * (cg % 2) + 1024]
                    elif cg == 32:
                        dst = mk.B_fm[:, 3:1027]
                    else:
                        dst = mk.C_fm[:, 3:1027]
                    sc.activation(out=dst, in_=mk.scr32[:, :], func=AF.Silu).then_inc(mk.s_act.h, 1)
                mk.s_act.n += 1
                convacc_free = mk.s_act.now()

                if cg < 32:
                    @blk.tensor
                    def _(t, cg=cg, pf=list(ptr_free)):
                        t.wait_ge(mk.s_act.h, mk.s_act.n)
                        src = mk.Sbf_a[:, 1024 * (cg % 2): 1024 * (cg % 2) + 1024]
                        for half in range(2):
                            bank = mk.ptr[half]
                            wait_evt(t, pf[half])
                            last = None
                            for j4 in range(4):
                                tt = half * 4 + j4
                                last = t.transpose(bank[0:128, j4 * 128: (j4 + 1) * 128],
                                                   src[:, tt * 128: (tt + 1) * 128],
                                                   mk.ident_b[:, :])
                            last.then_inc(mk.s_pe.h, 1)
                    pe_pre = mk.s_pe.n
                    mk.s_pe.n += 2
                    xsfm_free[cg % 2] = (mk.s_pe, mk.s_pe.n)

                    @blk.vector
                    def _(v, cg=cg, pe_pre=pe_pre, w=ubuf_free):
                        for half in range(2):
                            bank = mk.ptr[half]
                            v.wait_ge(mk.s_pe.h, pe_pre + half + 1)
                            if half == 0:
                                wait_evt(v, w)
                            last = None
                            for j4 in range(4):
                                tt = half * 4 + j4
                                last = v.tensor_copy(mk.Sbf_a[:, 2048 + 128 * tt: 2048 + 128 * (tt + 1)],
                                                     bank[0:128, j4 * 128: (j4 + 1) * 128])
                            last.then_inc(mk.s_dve.h, 1)
                    mk.s_dve.n += 2
                    ptr_free[0] = (mk.s_dve, mk.s_dve.n - 1)
                    ptr_free[1] = (mk.s_dve, mk.s_dve.n)

                    @blk.sync
                    def _(sync, cg=cg):
                        sync.wait_ge(mk.s_dve.h, mk.s_dve.n)
                        for tt in range(8):
                            sync.dma_start(
                                out=mk.xs_scr[128 * tt: 128 * (tt + 1), 128 * cg: 128 * (cg + 1)],
                                in_=mk.Sbf_a[:, 2048 + 128 * tt: 2048 + 128 * (tt + 1)]).then_inc(mk.s_st.h, 16)
                    mk.s_st.n += 8 * 16
                    ubuf_free = (mk.s_st, mk.s_st.n)

            if p < 8:
                self.unit_done[uidx - 2] = (mk.s_pe, mk.s_pe.n)
                self.unit_done[uidx - 1] = (mk.s_pe, mk.s_pe.n)

        for g2 in range(2):
            bank_i = g2 % 3

            @blk.tensor
            def _(t, g2=g2, units=units, bf=bank_free[bank_i], bank_i=bank_i):
                wait_evt(t, bf)
                last = None
                for h in range(2):
                    wait_evt(t, units[h][1])
                    ub = units[h][0]
                    for kc in range(16):
                        w_sl = mk.Wbuf[:, ub + kc * 320 + 256: ub + kc * 320 + 320]
                        last = t.matmul(mk.pb[bank_i][0:NH, :], w_sl,
                                        mk.xnT_sl(16 * h + kc, 3 + 512 * g2, 512),
                                        start=(h == 0 and kc == 0), stop=(h == 1 and kc == 15))
                last.then_inc(mk.s_pe.h, 1)
            mk.s_pe.n += 1

            @blk.vector
            def _(v, g2=g2, bank_i=bank_i):
                v.wait_ge(mk.s_pe.h, mk.s_pe.n)
                v.scalar_tensor_tensor(out=mk.cum[:, 512 * g2: 512 * (g2 + 1)],
                                       in0=mk.pb[bank_i][0:NH, :], scalar=mk.mup_dt[:, 0:1],
                                       in1=mk.r2bc[0:NH, 3 + 512 * g2: 3 + 512 * (g2 + 1)],
                                       op0=OP.mult, op1=OP.mult).then_inc(mk.s_dve.h, 1)
            mk.s_dve.n += 1
            bank_free[bank_i] = mk.s_dve.now()
        self.unit_done[uidx - 2] = (mk.s_pe, mk.s_pe.n)
        self.unit_done[uidx - 1] = (mk.s_pe, mk.s_pe.n)

        @blk.tensor
        def _(t, pf=list(ptr_free)):
            t.wait_ge(mk.s_act.h, mk.s_act.n)
            for half in range(2):
                bank = mk.ptr[half]
                wait_evt(t, pf[half])
                last = None
                for j4 in range(4):
                    tt = half * 4 + j4
                    last = t.transpose(bank[0:128, j4 * 128: (j4 + 1) * 128],
                                       mk.B_fm[:, 3 + tt * 128: 3 + (tt + 1) * 128],
                                       mk.ident_b[:, :])
                last.then_inc(mk.s_pe.h, 1)
        pe_pre = mk.s_pe.n
        mk.s_pe.n += 2

        @blk.vector
        def _(v, pe_pre=pe_pre):
            for half in range(2):
                bank = mk.ptr[half]
                v.wait_ge(mk.s_pe.h, pe_pre + half + 1)
                last = None
                for j4 in range(4):
                    tt = half * 4 + j4
                    last = v.tensor_copy(mk.B_tok[:, 128 * tt: 128 * (tt + 1)],
                                         bank[0:128, j4 * 128: (j4 + 1) * 128])
                last.then_inc(mk.s_dve.h, 1)
        mk.s_dve.n += 2
        ptr_free[0] = (mk.s_dve, mk.s_dve.n - 1)
        ptr_free[1] = (mk.s_dve, mk.s_dve.n)
        self.ptr_free = ptr_free

        for k in range(8):
            @blk.tensor
            def _(t, k=k, w=self.pscan_free):
                wait_evt(t, w)
                t.matmul(mk.pscan[:, 0:128], mk.B_fm[:, 3 + 128 * k: 3 + 128 * (k + 1)],
                         mk.C_fm[:, 3 + 128 * k: 3 + 128 * (k + 1)],
                         start=True, stop=True).then_inc(mk.s_pe.h, 1)
            mk.s_pe.n += 1

            @blk.vector
            def _(v, k=k):
                v.wait_ge(mk.s_pe.h, mk.s_pe.n)
                v.scalar_tensor_tensor(out=mk.GT_all[:, 128 * k: 128 * (k + 1)],
                                       in0=mk.pscan[:, 0:128], scalar=1.0, in1=mk.pen[:, :],
                                       op0=OP.mult, op1=OP.mult).then_inc(mk.s_dve.h, 1)
            mk.s_dve.n += 1
            self.pscan_free = mk.s_dve.now()

        self.bank_free = bank_free
        self.paux_free2 = paux_free
        self.uidx = uidx
        if self.debug and self.stage == 2:
            self.dbg_items += [("B_fm", self.B_fm), ("C_fm", self.C_fm), ("masks", self.masks),
                               ("dt_raw", self.cum), ("B_tok", self.B_tok),
                               ("GT_all", self.GT_all)]

    def p3_dtchain(self):
        mk = self
        blk = self.block

        @blk.scalar
        def _(sc, dve_need=mk.s_dve.n):
            sc.wait_ge(mk.s_dve.h, dve_need)
            sc.activation(out=mk.dt_sb[:, :], in_=mk.cum[:, :], func=AF.Abs,
                          bias=mk.dtbcol[:, 0:1])
            sc.drain()
            sc.activation(out=mk.dt_sb[:, :], in_=mk.dt_sb[:, :], func=AF.Exp, scale=-1.0)
            sc.drain()
            sc.activation(out=mk.dt_sb[:, :], in_=mk.dt_sb[:, :], func=AF.Ln,
                          bias=1.0).then_inc(mk.s_act.h, 1)
        mk.s_act.n += 1

        @blk.vector
        def _(v):
            v.wait_ge(mk.s_act.h, mk.s_act.n)
            v.tensor_scalar(out=mk.cs2[:, :], in0=mk.cum[:, :], scalar1=mk.dtbcol[:, 0:1],
                            scalar2=0.0, op0=OP.add, op1=OP.max)
            v.drain()
            v.tensor_tensor(out=mk.dt_sb[:, :], in0=mk.cs2[:, :], in1=mk.dt_sb[:, :], op=OP.add)
            v.engine_nop().then_inc(mk.s_dve.h, 1)
        mk.s_dve.n += 1

        @blk.scalar
        def _(sc):
            sc.wait_ge(mk.s_dve.h, mk.s_dve.n)
            sc.activation(out=mk.cs2[:, :], in_=mk.dt_sb[:, :], func=AF.Ln).then_inc(mk.s_act.h, 1)
        mk.s_act.n += 1

        @blk.vector
        def _(v):
            v.wait_ge(mk.s_act.h, mk.s_act.n)
            v.tensor_scalar(out=mk.scr32[0:NH, :], in0=mk.dt_sb[:, :], scalar1=mk.Acol[:, 0:1],
                            scalar2=None, op0=OP.mult, op1=OP.bypass)
            v.drain()
            for k in range(8):
                v.tensor_tensor_scan(out=mk.cum[:, 128 * k: 128 * (k + 1)],
                                     data0=mk.scr32[0:NH, 128 * k: 128 * (k + 1)],
                                     data1=mk.zeros64[:, :], initial=0.0,
                                     op0=OP.add, op1=OP.add)
            v.drain()
            v.tensor_tensor(out=mk.cs2[:, :], in0=mk.cum[:, :], in1=mk.cs2[:, :],
                            op=OP.subtract).then_inc(mk.s_dve.h, 1)
        mk.s_dve.n += 1

        @blk.sync
        def _(sync):
            sync.wait_ge(mk.s_dve.h, mk.s_dve.n)
            sync.dma_start(out=mk.cum_scr[:, :], in_=mk.cum[:, :]).then_inc(mk.s_st.h, 16)
        mk.s_st.n += 16
        self.cum_scr_ready = (mk.s_st, mk.s_st.n)

        self.p3_done_dve = mk.s_dve.now()
        if self.debug and self.stage == 3:
            self.dbg_items += [("dt_sb", self.dt_sb), ("cum", self.cum), ("cs2", self.cs2)]

    def p4_passA(self):
        mk = self
        blk = self.block

        @blk.gpsimd
        def _(g, w=self.ln_done_dve):
            wait_evt(g, w)
            g.memset(mk.S32b[:, :], 0.0)
            g.engine_nop().then_inc(mk.s_pool.h, 1)
        mk.s_pool.n += 1

        xs_all_stored = (mk.s_st, mk.s_st.n)
        sbfa_free = None
        sbfb_free = None
        wq_free = None

        for k in range(8):
            @blk.sync
            def _(sync, k=k, w1=xs_all_stored, w2=sbfa_free):
                wait_evts(sync, w1, w2)
                sync.dma_start(out=mk.Sbf_a[:, :],
                               in_=mk.xs_scr[128 * k: 128 * (k + 1), :]).then_inc(mk.s_ld.h, 16)
            mk.s_ld.n += 16
            xs_ld = mk.s_ld.n

            @blk.vector
            def _(v, k=k, w=wq_free):
                wait_evt(v, w)
                v.tensor_scalar(out=mk.wq[:, 0:T], in0=mk.cum[:, 128 * k: 128 * (k + 1)],
                                scalar1=mk.cum[:, 128 * k + 127: 128 * k + 128],
                                scalar2=None, op0=OP.subtract, op1=OP.bypass).then_inc(mk.s_dve.h, 1)
            mk.s_dve.n += 1

            @blk.scalar
            def _(sc, k=k):
                sc.wait_ge(mk.s_dve.h, mk.s_dve.n)
                sc.activation(out=mk.wq[:, 0:T], in_=mk.wq[:, 0:T], func=AF.Exp,
                              scale=-1.0)
                sc.drain()
                sc.activation(out=mk.Pcol[:, 1:2], in_=mk.cum[:, 128 * k + 127: 128 * k + 128],
                              func=AF.Exp).then_inc(mk.s_act.h, 1)
            mk.s_act.n += 1

            @blk.vector
            def _(v, k=k):
                v.wait_ge(mk.s_act.h, mk.s_act.n)
                v.tensor_tensor(out=mk.wq[:, 0:T], in0=mk.dt_sb[:, 128 * k: 128 * (k + 1)],
                                in1=mk.wq[:, 0:T], op=OP.mult)
                v.drain()
                v.tensor_tensor(out=mk.Pcol[:, 0:1], in0=mk.Pcol[:, 0:1],
                                in1=mk.Pcol[:, 1:2], op=OP.mult).then_inc(mk.s_dve.h, 1)
            mk.s_dve.n += 1

            @blk.tensor
            def _(t, k=k, w=self.pscan_free):
                t.wait_ge(mk.s_dve.h, mk.s_dve.n)
                wait_evt(t, w)
                t.transpose(mk.pscan[0:128, 0:NH], mk.wq[:, 0:T],
                            mk.ident_f[0:NH, 0:NH]).then_inc(mk.s_pe.h, 1)
                t.transpose(mk.pscan[0:1, 256:256 + NH],
                            mk.cum[:, 128 * k + 127: 128 * k + 128],
                            mk.ident_f[0:NH, 0:NH]).then_inc(mk.s_pe.h, 1)
            mk.s_pe.n += 2
            wq_free = (mk.s_pe, mk.s_pe.n)

            @blk.vector
            def _(v, k=k):
                v.wait_ge(mk.s_pe.h, mk.s_pe.n)
                v.tensor_copy(mk.wqT[:, :], mk.pscan[0:128, 0:NH])
                v.tensor_copy(mk.row1[0:1, 0:NH], mk.pscan[0:1, 256:256 + NH]).then_inc(mk.s_dve.h, 1)
            mk.s_dve.n += 1

            @blk.scalar
            def _(sc):
                sc.wait_ge(mk.s_dve.h, mk.s_dve.n)
                sc.activation(out=mk.row1[0:1, 0:NH], in_=mk.row1[0:1, 0:NH],
                              func=AF.Exp).then_inc(mk.s_act.h, 1)
            mk.s_act.n += 1

            @blk.tensor
            def _(t):
                t.wait_ge(mk.s_act.h, mk.s_act.n)
                t.matmul(mk.pscan[:, 256:256 + NH], mk.ones_f[:, :], mk.row1[0:1, 0:NH],
                         start=True, stop=True).then_inc(mk.s_pe.h, 1)
            mk.s_pe.n += 1

            @blk.vector
            def _(v, k=k):
                v.wait_ge(mk.s_pe.h, mk.s_pe.n)
                v.tensor_copy(mk.aTbc[:, :], mk.pscan[:, 256:256 + NH])
                v.drain()
                if k < 7:
                    v.tensor_tensor(out=mk.Qbc[:, NH * (k + 1): NH * (k + 2)],
                                    in0=mk.Qbc[:, NH * k: NH * (k + 1)], in1=mk.aTbc[:, :],
                                    op=OP.mult)
                v.engine_nop().then_inc(mk.s_dve.h, 1)
            mk.s_dve.n += 1
            self.pscan_free = mk.s_dve.now()

            @blk.vector
            def _(v, k=k, xs_ld=xs_ld, w=sbfb_free):
                v.wait_ge(mk.s_ld.h, xs_ld)
                wait_evt(v, w)
                last = None
                for h in range(NH):
                    last = v.tensor_scalar(out=mk.Sbf_b[:, 64 * h: 64 * (h + 1)],
                                           in0=mk.Sbf_a[:, 64 * h: 64 * (h + 1)],
                                           scalar1=mk.wqT[:, h: h + 1], scalar2=None,
                                           op0=OP.mult, op1=OP.bypass)
                last.then_inc(mk.s_dve.h, 1)
            mk.s_dve.n += 1
            sbfa_free = (mk.s_dve, mk.s_dve.n)
            xw_ready = mk.s_dve.n

            pY_free = None
            for p in range(8):
                @blk.tensor
                def _(t, k=k, p=p, xw_ready=xw_ready, w=pY_free):
                    t.wait_ge(mk.s_dve.h, xw_ready)
                    wait_evt(t, w)
                    t.matmul(mk.pY[:, :], mk.B_tok[:, 128 * k: 128 * (k + 1)],
                             mk.Sbf_b[:, 512 * p: 512 * (p + 1)],
                             start=True, stop=True).then_inc(mk.s_pe.h, 1)
                mk.s_pe.n += 1

                @blk.vector
                def _(v, k=k, p=p):
                    v.wait_ge(mk.s_pe.h, mk.s_pe.n)
                    last = None
                    for hh in range(8):
                        h = 8 * p + hh
                        sl = mk.S32b[:, 64 * h: 64 * (h + 1)]
                        last = v.scalar_tensor_tensor(out=sl, in0=mk.pY[:, 64 * hh: 64 * (hh + 1)],
                                                      scalar=1.0, in1=sl, op0=OP.mult, op1=OP.add)
                    v.drain()
                    for hh in range(8):
                        h = 8 * p + hh
                        sl = mk.S32b[:, 64 * h: 64 * (h + 1)]
                        last = v.tensor_scalar(out=sl, in0=sl, scalar1=mk.aTbc[:, h: h + 1],
                                               scalar2=None, op0=OP.mult, op1=OP.bypass)
                    last.then_inc(mk.s_dve.h, 1)
                mk.s_dve.n += 1
                pY_free = (mk.s_dve, mk.s_dve.n)
            sbfb_free = (mk.s_pe, mk.s_pe.n)

            if k < 7:
                @blk.gpsimd
                def _(g, k=k):
                    g.wait_ge(mk.s_dve.h, mk.s_dve.n)
                    g.dma_start(out=mk.sloc_scr[k, :, :], in_=mk.S32b[:, :]).then_inc(mk.s_gst.h, 16)
                mk.s_gst.n += 16

        @blk.gpsimd
        def _(g, dve_need=mk.s_dve.n):
            g.wait_ge(mk.s_dve.h, dve_need)
            g.dma_start(out=mk.ag_in[:, :], in_=mk.S32b[:, :]).then_inc(mk.s_gst.h, 16)
            g.dma_start(out=mk.agp_in[:, :], in_=mk.Pcol[:, 0:1]).then_inc(mk.s_gst.h, 16)
            g.wait_ge(mk.s_gst.h, mk.s_gst.n + 32)
            g.collective_compute("AllGather", OP.bypass,
                                 replica_groups=[[0, 1, 2, 3], [4, 5, 6, 7]],
                                 ins=[mk.ag_in[:, :]], outs=[mk.ag_out[:, :]]).then_inc(mk.s_cc.h)
            g.collective_compute("AllGather", OP.bypass,
                                 replica_groups=[[0, 1, 2, 3], [4, 5, 6, 7]],
                                 ins=[mk.agp_in[:, :]], outs=[mk.agp_out[:, :]]).then_inc(mk.s_cc.h)
        mk.s_gst.n += 32
        mk.s_cc.n += 2
        self.sloc_stored = (mk.s_gst, mk.s_gst.n)
        self.p4_sbfa = sbfa_free
        self.p4_sbfb = sbfb_free
        if self.debug and self.stage == 4:
            self.dbg_items += [("h_out", self.S32b), ("Qbc", self.Qbc)]

    def p5_passB(self):
        mk = self
        blk = self.block

        @blk.sync
        def _(sync, cc_need=mk.s_cc.n):
            sync.wait_ge(mk.s_cc.h, cc_need)
            sync.dma_start(out=mk.scr32[0:4, 0:NH],
                           in_=mk.agp_out.rearrange("(r h) o -> r (h o)", r=4)
                           ).then_inc(mk.s_ld.h, 16)
        mk.s_ld.n += 16

        @blk.tensor
        def _(t, w=self.pscan_free, ld=mk.s_ld.n):
            t.wait_ge(mk.s_ld.h, ld)
            wait_evt(t, w)
            t.transpose(mk.pscan[0:NH, 0:4], mk.scr32[0:4, 0:NH],
                        mk.ident_f[0:4, 0:4]).then_inc(mk.s_pe.h, 1)
        mk.s_pe.n += 1

        @blk.vector
        def _(v):
            v.wait_ge(mk.s_pe.h, mk.s_pe.n)
            v.tensor_copy(mk.gsm[:, 8:12], mk.pscan[0:NH, 0:4])
            v.drain()
            for m in range(4):
                v.tensor_tensor(out=mk.gsm[:, 12 + m:13 + m], in0=mk.gsm[:, 8 + m:9 + m],
                                in1=mk.gsm[:, 4 + m:5 + m], op=OP.mult)
            v.drain()
            for m in range(4):
                v.tensor_scalar(out=mk.wq[0:NH, m:m + 1], in0=mk.gsm[:, 4 + m:5 + m],
                                scalar1=-1.0, scalar2=1.0, op0=OP.mult, op1=OP.add)
            v.drain()
            for m in range(4):
                v.tensor_tensor(out=mk.gsm[:, 12 + m:13 + m], in0=mk.gsm[:, 12 + m:13 + m],
                                in1=mk.wq[0:NH, m:m + 1], op=OP.add)
            v.drain()
            v.memset(mk.wq[0:NH, 4:5], 1.0)
            v.drain()
            for r in (3, 2, 1, 0):
                v.tensor_tensor(out=mk.wq[0:NH, 8 + r:9 + r], in0=mk.gsm[:, 0 + r:1 + r],
                                in1=mk.wq[0:NH, 4:5], op=OP.mult)
                v.drain()
                v.tensor_tensor(out=mk.wq[0:NH, 4:5], in0=mk.wq[0:NH, 4:5],
                                in1=mk.gsm[:, 12 + r:13 + r], op=OP.mult)
                v.drain()
            v.engine_nop().then_inc(mk.s_dve.h, 1)
        mk.s_dve.n += 1

        for r in range(4):
            @blk.tensor
            def _(t, r=r, dve_need=mk.s_dve.n, w=self.pscan_free):
                t.wait_ge(mk.s_dve.h, dve_need)
                wait_evt(t, w)
                t.transpose(mk.pscan[0:1, 256:256 + NH], mk.wq[0:NH, 8 + r:9 + r],
                            mk.ident_f[0:NH, 0:NH]).then_inc(mk.s_pe.h, 1)
            mk.s_pe.n += 1

            @blk.vector
            def _(v):
                v.wait_ge(mk.s_pe.h, mk.s_pe.n)
                v.tensor_copy(mk.row1[0:1, 0:NH], mk.pscan[0:1, 256:256 + NH]).then_inc(mk.s_dve.h, 1)
            mk.s_dve.n += 1

            @blk.tensor
            def _(t):
                t.wait_ge(mk.s_dve.h, mk.s_dve.n)
                t.matmul(mk.pscan[:, 256:256 + NH], mk.ones_f[:, :], mk.row1[0:1, 0:NH],
                         start=True, stop=True).then_inc(mk.s_pe.h, 1)
            mk.s_pe.n += 1

            @blk.vector
            def _(v, r=r):
                v.wait_ge(mk.s_pe.h, mk.s_pe.n)
                v.tensor_copy(mk.wbc[:, NH * r: NH * (r + 1)],
                              mk.pscan[:, 256:256 + NH]).then_inc(mk.s_dve.h, 1)
            mk.s_dve.n += 1
            self.pscan_free = mk.s_dve.now()

        @blk.gpsimd
        def _(g, dve_need=mk.s_dve.n):
            g.wait_ge(mk.s_dve.h, dve_need)
            g.memset(mk.S32a[:, :], 0.0)
            g.engine_nop().then_inc(mk.s_pool.h, 1)
        mk.s_pool.n += 1

        hp_cons = []
        for r in range(4):
            for p in range(8):
                j = r * 8 + p

                @blk.gpsimd
                def _(g, r=r, p=p, j=j, pool_need=mk.s_pool.n,
                      w=(hp_cons[j - 2] if j >= 2 else None)):
                    g.wait_ge(mk.s_pool.h, pool_need)
                    wait_evt(g, w)
                    g.dma_start(out=mk.lam[:, 512 * (j % 2): 512 * (j % 2) + 512],
                                in_=mk.ag_out[128 * r: 128 * (r + 1), 512 * p: 512 * (p + 1)]
                                ).then_inc(mk.s_w.h, 16)
                mk.s_w.n += 16

                @blk.vector
                def _(v, r=r, p=p, j=j, wn=mk.s_w.n):
                    v.wait_ge(mk.s_w.h, wn)
                    last = None
                    for hh in range(8):
                        h = 8 * p + hh
                        sl = mk.S32a[:, 64 * h: 64 * (h + 1)]
                        last = v.scalar_tensor_tensor(
                            out=sl,
                            in0=mk.lam[:, 512 * (j % 2) + 64 * hh: 512 * (j % 2) + 64 * (hh + 1)],
                            scalar=mk.wbc[:, NH * r + h: NH * r + h + 1], in1=sl,
                            op0=OP.mult, op1=OP.add)
                    last.then_inc(mk.s_dve.h, 1)
                mk.s_dve.n += 1
                hp_cons.append((mk.s_dve, mk.s_dve.n))

        @blk.gpsimd
        def _(g, dve_need=mk.s_dve.n):
            g.wait_ge(mk.s_dve.h, dve_need)
            g.memset(mk.S32b[:, :], 0.0)
            g.engine_nop().then_inc(mk.s_pool.h, 1)
        mk.s_pool.n += 1
        sloc_ready = (mk.s_pool, mk.s_pool.n)

        sbfa_free = self.p4_sbfa
        sbfb_free = self.p4_sbfb
        hs_cons = [None, None]
        rows_cons = [None, None]
        cumbc_cons = [self.bank_free[2], self.paux_free2]
        exp_cons = None
        intra_done = None
        intra_bank_cons = [self.bank_free[0], self.bank_free[1]]
        pY_cons = None
        transT_cons = None
        zw_cons2 = [None, None]
        ptr_free = list(self.ptr_free)

        for k in range(8):
            @blk.gpsimd
            def _(g, k=k, w1=sbfa_free, w2=sloc_ready, w3=self.sloc_stored,
                  w4=(mk.s_st, mk.s_st.n)):
                wait_evts(g, w1, w2, w3, w4)
                if k > 0:
                    g.dma_start(out=mk.S32b[:, :],
                                in_=mk.sloc_scr[k - 1, :, :]).then_inc(mk.s_w.h, 16)
                g.dma_start(out=mk.Sbf_a[:, :],
                            in_=mk.xs_scr[128 * k: 128 * (k + 1), :]).then_inc(mk.s_w.h, 16)
            mk.s_w.n += 32 if k > 0 else 16
            chunk_ld = mk.s_w.n

            @blk.gpsimd
            def _(g, k=k):
                g.dma_start(out=mk.woutbf[512 * k: 512 * (k + 1), :],
                            in_=mk.Wout[512 * k: 512 * (k + 1), :]).then_inc(mk.s_gst.h, 16)
            mk.s_gst.n += 16

            @blk.tensor
            def _(t, k=k, w1=self.pscan_free, w2=transT_cons, w3=self.p3_done_dve):
                wait_evts(t, w1, w2, w3)
                t.transpose(mk.pscan[0:128, 256:256 + NH],
                            mk.cum[:, 128 * k: 128 * (k + 1)],
                            mk.ident_f[0:NH, 0:NH])
                t.transpose(mk.pscan[0:128, 320:320 + NH],
                            mk.cs2[:, 128 * k: 128 * (k + 1)],
                            mk.ident_f[0:NH, 0:NH]).then_inc(mk.s_pe.h, 1)
            mk.s_pe.n += 1

            @blk.vector
            def _(v, k=k):
                v.wait_ge(mk.s_pe.h, mk.s_pe.n)
                v.tensor_copy(mk.transT[:, 0:NH], mk.pscan[0:128, 256:256 + NH])
                v.tensor_scalar(out=mk.transT[:, NH:2 * NH], in0=mk.pscan[0:128, 320:320 + NH],
                                scalar1=-1.0, scalar2=None, op0=OP.mult, op1=OP.bypass).then_inc(mk.s_dve.h, 1)
            mk.s_dve.n += 1
            self.pscan_free = mk.s_dve.now()
            transT_ready = mk.s_dve.n

            @blk.scalar
            def _(sc, k=k, tr=transT_ready):
                sc.wait_ge(mk.s_dve.h, tr)
                sc.activation(out=mk.ecumT[:, :], in_=mk.transT[:, 0:NH],
                              func=AF.Exp).then_inc(mk.s_act.h, 1)
            mk.s_act.n += 1
            ecum_ready = mk.s_act.n

            for p in range(8):
                rows_ld = [0, 0]
                for gq in range(2):
                    rslot = (2 * p + gq) % 2

                    @blk.sync
                    def _(sync, k=k, p=p, gq=gq, rslot=rslot,
                          w1=self.cum_scr_ready, w2=rows_cons[rslot]):
                        wait_evts(sync, w1, w2)
                        sync.dma_start(out=mk.rows[0:1, 512 * rslot: 512 * rslot + 512]
                                       .rearrange("o (h t) -> o h t", h=4),
                                       in_=mk.cum_scr[8 * p + 4 * gq: 8 * p + 4 * (gq + 1),
                                                      128 * k: 128 * (k + 1)]
                                       .rearrange("(o h) t -> o h t", o=1)
                                       ).then_inc(mk.s_rows.h, 16)
                    mk.s_rows.n += 16
                    rows_ld[gq] = mk.s_rows.n

                hslot = p % 2

                @blk.vector
                def _(v, k=k, p=p, hslot=hslot, chunk_ld=chunk_ld, w=hs_cons[hslot]):
                    v.wait_ge(mk.s_w.h, chunk_ld)
                    wait_evt(v, w)
                    last = None
                    for hh in range(8):
                        h = 8 * p + hh
                        last = v.scalar_tensor_tensor(
                            out=mk.hsbuf[:, 512 * hslot + 64 * hh: 512 * hslot + 64 * (hh + 1)],
                            in0=mk.S32a[:, 64 * h: 64 * (h + 1)],
                            scalar=mk.Qbc[:, NH * k + h: NH * k + h + 1],
                            in1=mk.S32b[:, 64 * h: 64 * (h + 1)], op0=OP.mult, op1=OP.add)
                    last.then_inc(mk.s_dve.h, 1)
                mk.s_dve.n += 1
                hs_ready = mk.s_dve.n

                @blk.tensor
                def _(t, k=k, hslot=hslot, hs_ready=hs_ready, w=pY_cons):
                    t.wait_ge(mk.s_dve.h, hs_ready)
                    wait_evt(t, w)
                    t.matmul(mk.pY[:, :], mk.C_fm[:, 3 + 128 * k: 3 + 128 * (k + 1)],
                             mk.hsbuf[:, 512 * hslot: 512 * hslot + 512],
                             start=True, stop=True).then_inc(mk.s_pe.h, 1)
                mk.s_pe.n += 1
                hs_cons[hslot] = (mk.s_pe, mk.s_pe.n)
                inter_done = mk.s_pe.n

                bank_i = (k * 8 + p) % 2

                @blk.tensor
                def _(t, k=k, p=p, rl=max(rows_ld), w1=cumbc_cons[0], w1b=cumbc_cons[1],
                      w2=(intra_bank_cons[bank_i])):
                    t.wait_ge(mk.s_rows.h, rl)
                    wait_evts(t, w1, w1b, w2)
                    rs_a = (2 * p) % 2
                    rs_b = (2 * p + 1) % 2
                    t.matmul(mk.pb[2][:, 0:512], mk.ones_f[:, :],
                             mk.rows[0:1, 512 * rs_a: 512 * rs_a + 512], start=True, stop=True)
                    t.matmul(mk.paux[:, 0:512], mk.ones_f[:, :],
                             mk.rows[0:1, 512 * rs_b: 512 * rs_b + 512],
                             start=True, stop=True).then_inc(mk.s_pe.h, 1)
                mk.s_pe.n += 1
                rows_cons[0] = rows_cons[1] = (mk.s_pe, mk.s_pe.n)

                @blk.scalar
                def _(sc, k=k, p=p, pe_need=mk.s_pe.n, w=intra_done):
                    sc.wait_ge(mk.s_pe.h, pe_need)
                    wait_evt(sc, w)
                    last = None
                    for hh in range(8):
                        h = 8 * p + hh
                        cb = mk.pb[2] if hh < 4 else mk.paux
                        last = sc.activation(out=mk.lam[:, 128 * hh: 128 * (hh + 1)],
                                             in_=cb[:, 128 * (hh % 4): 128 * (hh % 4 + 1)],
                                             func=AF.Exp, bias=mk.transT[:, NH + h: NH + h + 1])
                    last.then_inc(mk.s_act.h, 1)
                mk.s_act.n += 1
                cumbc_cons[0] = cumbc_cons[1] = (mk.s_act, mk.s_act.n)
                exp_cons = (mk.s_act, mk.s_act.n)

                @blk.vector
                def _(v, k=k):
                    v.wait_ge(mk.s_act.h, mk.s_act.n)
                    last = None
                    for hh in range(8):
                        last = v.scalar_tensor_tensor(out=mk.lam[:, 128 * hh: 128 * (hh + 1)],
                                                      in0=mk.lam[:, 128 * hh: 128 * (hh + 1)],
                                                      scalar=1e30, in1=mk.GT_all[:, 128 * k: 128 * (k + 1)],
                                                      op0=OP.min, op1=OP.mult)
                    last.then_inc(mk.s_dve.h, 1)
                mk.s_dve.n += 1

                @blk.tensor
                def _(t, k=k, p=p, bank_i=bank_i, dve_need=mk.s_dve.n):
                    t.wait_ge(mk.s_dve.h, dve_need)
                    last = None
                    for hh in range(8):
                        last = t.matmul(mk.pb[bank_i][:, 64 * hh: 64 * (hh + 1)],
                                        mk.lam[:, 128 * hh: 128 * (hh + 1)],
                                        mk.Sbf_a[:, 64 * (8 * p + hh): 64 * (8 * p + hh + 1)],
                                        start=(hh == 0), stop=(hh == 7))
                    last.then_inc(mk.s_pe.h, 1)
                mk.s_pe.n += 1
                intra_done = (mk.s_pe, mk.s_pe.n)

                @blk.vector
                def _(v, k=k, p=p, bank_i=bank_i, ecum_ready=ecum_ready):
                    v.wait_ge(mk.s_pe.h, mk.s_pe.n)
                    v.wait_ge(mk.s_act.h, ecum_ready)
                    last = None
                    for hh in range(8):
                        h = 8 * p + hh
                        sl = mk.Sbf_a[:, 64 * h: 64 * (h + 1)]
                        last = v.scalar_tensor_tensor(out=sl, in0=sl, scalar=mk.Dbc[:, h: h + 1],
                                                      in1=mk.pb[bank_i][:, 64 * hh: 64 * (hh + 1)],
                                                      op0=OP.mult, op1=OP.add)
                    v.drain()
                    for hh in range(8):
                        h = 8 * p + hh
                        sl = mk.Sbf_a[:, 64 * h: 64 * (h + 1)]
                        last = v.scalar_tensor_tensor(out=sl, in0=mk.pY[:, 64 * hh: 64 * (hh + 1)],
                                                      scalar=mk.ecumT[:, h: h + 1], in1=sl,
                                                      op0=OP.mult, op1=OP.add)
                    last.then_inc(mk.s_dve.h, 1)
                mk.s_dve.n += 1
                pY_cons = (mk.s_dve, mk.s_dve.n)
                intra_bank_cons[bank_i] = (mk.s_dve, mk.s_dve.n)
            transT_cons = (mk.s_dve, mk.s_dve.n)

            @blk.gpsimd
            def _(g, k=k, w=sbfb_free):
                wait_evt(g, w)
                g.dma_start(out=mk.Sbf_b[:, :],
                            in_=mk.z_scr[128 * k: 128 * (k + 1), :]).then_inc(mk.s_w.h, 16)
            mk.s_w.n += 16

            @blk.scalar
            def _(sc, k=k, ld=mk.s_w.n):
                sc.wait_ge(mk.s_w.h, ld)
                sc.activation(out=mk.Sbf_b[:, :], in_=mk.Sbf_b[:, :],
                              func=AF.Silu).then_inc(mk.s_act.h, 1)
            mk.s_act.n += 1

            @blk.vector
            def _(v, k=k):
                v.wait_ge(mk.s_act.h, mk.s_act.n)
                v.tensor_tensor(out=mk.Sbf_b[:, :], in0=mk.Sbf_a[:, :], in1=mk.Sbf_b[:, :],
                                op=OP.mult)
                v.drain()
                v.scalar_tensor_tensor(out=mk.Sbf_a[:, :], in0=mk.Sbf_b[:, :], scalar=1.0,
                                       in1=mk.Sbf_b[:, :], op0=OP.mult, op1=OP.mult,
                                       accum_out=mk.stat[:, 11:12])
                v.drain()
                v.tensor_scalar(out=mk.stat[:, 12:13], in0=mk.stat[:, 11:12],
                                scalar1=1.0 / 4096, scalar2=EPS, op0=OP.mult, op1=OP.add)
                v.engine_nop().then_inc(mk.s_dve.h, 1)
            mk.s_dve.n += 1
            sbfa_free = (mk.s_dve, mk.s_dve.n)

            @blk.scalar
            def _(sc, k=k):
                sc.wait_ge(mk.s_dve.h, mk.s_dve.n)
                sc.activation(out=mk.stat[:, 13:14], in_=mk.stat[:, 12:13], func=AF.Ln)
                sc.drain()
                sc.activation(out=mk.stat[:, 14:15], in_=mk.stat[:, 13:14], func=AF.Exp,
                              scale=-0.5).then_inc(mk.s_act.h, 1)
            mk.s_act.n += 1

            @blk.vector
            def _(v, k=k):
                v.wait_ge(mk.s_act.h, mk.s_act.n)
                v.tensor_scalar(out=mk.Sbf_b[:, :], in0=mk.Sbf_b[:, :],
                                scalar1=mk.stat[:, 14:15], scalar2=None,
                                op0=OP.mult, op1=OP.bypass).then_inc(mk.s_dve.h, 1)
            mk.s_dve.n += 1
            gn_ready = mk.s_dve.n

            for q in range(8):
                @blk.tensor
                def _(t, k=k, q=q, gn_ready=gn_ready, w=ptr_free[q % 2]):
                    t.wait_ge(mk.s_dve.h, gn_ready)
                    wait_evt(t, w)
                    bank = mk.ptr[q % 2]
                    last = None
                    for i in range(4):
                        c = 4 * q + i
                        last = t.transpose(bank[0:128, i * 128: (i + 1) * 128],
                                           mk.Sbf_b[:, 128 * c: 128 * (c + 1)],
                                           mk.ident_b[:, :])
                    last.then_inc(mk.s_pe.h, 1)
                mk.s_pe.n += 1

                @blk.vector
                def _(v, k=k, q=q, w=zw_cons2[q % 2]):
                    v.wait_ge(mk.s_pe.h, mk.s_pe.n)
                    wait_evt(v, w)
                    bank = mk.ptr[q % 2]
                    last = None
                    for i in range(4):
                        last = v.tensor_copy(
                            mk.Sbf_a[:, 512 * (q % 2) + 128 * i: 512 * (q % 2) + 128 * (i + 1)],
                            bank[0:128, i * 128: (i + 1) * 128])
                    last.then_inc(mk.s_dve.h, 1)
                mk.s_dve.n += 1
                ptr_free[q % 2] = (mk.s_dve, mk.s_dve.n)

                @blk.sync
                def _(sync, k=k, q=q):
                    sync.wait_ge(mk.s_dve.h, mk.s_dve.n)
                    sync.dma_start(out=mk.gnt_scr[k, q, :, :],
                                   in_=mk.Sbf_a[:, 512 * (q % 2): 512 * (q % 2) + 512]
                                   ).then_inc(mk.s_st.h, 16)
                mk.s_st.n += 16
                zw_cons2[q % 2] = (mk.s_st, mk.s_st.n)

            sbfb_free = (mk.s_pe, mk.s_pe.n)

        self.ptr_free = ptr_free
        self.intra_bank_cons = intra_bank_cons
        self.woutbf_ready = (mk.s_gst, mk.s_gst.n)
        self.cumbc_pb2_cons = cumbc_cons[0]
        self.cumbc_paux_cons = cumbc_cons[0]
        self.gnt_stored = (mk.s_st, mk.s_st.n)
        if self.debug and self.stage == 5:
            self.dbg_items += [("h_in", self.S32a), ("wbc", self.wbc),
                               ("gnt", self.gnt_scr),
                               ("zscr", self.z_scr),
                               ("xsscr", self.xs_scr)]

    def p6_experts(self):
        mk = self
        blk = self.block
        uidx = self.uidx
        bank_cons = {0: self.intra_bank_cons[0], 1: self.intra_bank_cons[1],
                     2: self.cumbc_pb2_cons}
        u_cons = [self.gnt_stored] * 4
        brow_cons = None
        acc_stored = None

        for p in range(8):
            for e in range(8):
                units = []
                for h in range(2):
                    src = mk.W_e[e, 2048 * h: 2048 * (h + 1), 512 * p: 512 * (p + 1)].rearrange(
                        "(kc p) c -> p kc c", p=128)
                    units.append(self.w_unit_load(uidx, src, 512, None))
                    uidx += 1

                @blk.gpsimd
                def _(g, p=p, e=e, w=brow_cons):
                    wait_evt(g, w)
                    g.dma_start(out=mk.brow[0:1, 0:512],
                                in_=mk.b_e[e, 512 * p: 512 * (p + 1)].rearrange("(o c) -> o c", o=1)
                                ).then_inc(mk.s_w.h, 16)
                mk.s_w.n += 16
                brow_ld = mk.s_w.n

                for t_i in range(NT):
                    bank_i = (p * 64 + e * 8 + t_i) % 3
                    bank = mk.pb[bank_i]
                    uslot = (e * 8 + t_i) % 4

                    @blk.tensor
                    def _(t, p=p, e=e, t_i=t_i, bank=bank, units=units, brow_ld=brow_ld,
                          w=bank_cons[bank_i]):
                        wait_evt(t, w)
                        t.wait_ge(mk.s_w.h, brow_ld)
                        t.matmul(bank[:, :], mk.ones_b[:, :], mk.brow[0:1, 0:512],
                                 start=True, stop=False)
                        for h in range(2):
                            wait_evt(t, units[h][1])
                            ub = units[h][0]
                            for kc in range(16):
                                last = t.matmul(bank[:, :], mk.xnT_sl(16 * h + kc, 3 + 128 * t_i, 128),
                                                mk.Wbuf[:, ub + kc * 512: ub + (kc + 1) * 512],
                                                start=False, stop=(h == 1 and kc == 15))
                        last.then_inc(mk.s_pe.h, 1)
                    mk.s_pe.n += 1
                    if t_i == NT - 1:
                        self.unit_done[uidx - 2] = (mk.s_pe, mk.s_pe.n)
                        self.unit_done[uidx - 1] = (mk.s_pe, mk.s_pe.n)
                        brow_cons = (mk.s_pe, mk.s_pe.n)

                    @blk.scalar
                    def _(sc, p=p, e=e, t_i=t_i, bank=bank, uslot=uslot, w=u_cons[uslot]):
                        sc.wait_ge(mk.s_pe.h, mk.s_pe.n)
                        wait_evt(sc, w)
                        sc.activation(out=mk.Sbf_a[:, 512 * uslot: 512 * (uslot + 1)],
                                      in_=bank[:, :], func=AF.Relu,
                                      scale=mk.masks[:, 8 * t_i + e: 8 * t_i + e + 1]
                                      ).then_inc(mk.s_act.h, 1)
                    mk.s_act.n += 1
                    bank_cons[bank_i] = (mk.s_act, mk.s_act.n)

                    @blk.vector
                    def _(v, e=e, t_i=t_i, uslot=uslot, w=acc_stored):
                        v.wait_ge(mk.s_act.h, mk.s_act.n)
                        usl = mk.Sbf_a[:, 512 * uslot: 512 * (uslot + 1)]
                        asl = mk.S32a[:, 512 * t_i: 512 * (t_i + 1)]
                        if e == 0:
                            wait_evt(v, w)
                            v.tensor_copy(asl, usl).then_inc(mk.s_dve.h, 1)
                        else:
                            v.tensor_tensor(out=asl, in0=asl, in1=usl, op=OP.add).then_inc(mk.s_dve.h, 1)
                    mk.s_dve.n += 1
                    u_cons[uslot] = (mk.s_dve, mk.s_dve.n)

            @blk.sync
            def _(sync, p=p):
                sync.wait_ge(mk.s_dve.h, mk.s_dve.n)
                sync.dma_start(out=mk.ffn_scr.rearrange("(t p) C -> p t C", p=128)[:, :, 512 * p: 512 * (p + 1)],
                               in_=mk.S32a.ap().rearrange("p (t c) -> p t c", t=8)).then_inc(mk.s_st.h, 16)
            mk.s_st.n += 16
            acc_stored = (mk.s_st, mk.s_st.n)

        self.bank_cons6 = bank_cons
        self.ffn_stored = acc_stored
        self.uidx = uidx
        self.p6_dve_done = (mk.s_dve, mk.s_dve.n)
        if self.debug and self.stage == 6:
            self.dbg_items += [("ffn", self.ffn_scr),
                               ("masks", self.masks)]

    def p7_outproj(self):
        mk = self
        blk = self.block
        uidx = self.uidx
        banks4 = [mk.pb[0], mk.pb[1], mk.pb[2], mk.paux]
        bank_cons = {0: self.bank_cons6[0], 1: self.bank_cons6[1], 2: self.bank_cons6[2],
                     3: self.cumbc_paux_cons}
        gt_cons = [self.p6_dve_done] * 4
        op_cons = [None, None]
        out_cons = [None, None]

        for cp in range(8):
            for tq in range(2):
                units = []
                for h in range(2):
                    srcw = mk.woutbf[2048 * h: 2048 * (h + 1), 512 * cp: 512 * (cp + 1)].rearrange(
                        "(kc p) c -> p kc c", p=128)
                    units.append(self.w_unit_load(uidx, srcw, 512, (mk.nw_c, 16 * h),
                                                  pre=self.woutbf_ready))
                    uidx += 1
                for half in range(2):
                    for ti4 in range(4):
                        t_i = 4 * tq + ti4
                        gslot = (half * 4 + ti4) % 4

                        @blk.sync
                        def _(sync, t_i=t_i, half=half, gslot=gslot,
                              w1=gt_cons[gslot], w2=self.gnt_stored):
                            wait_evts(sync, w1, w2)
                            for q4 in range(4):
                                q = 4 * half + q4
                                gt = mk.Sbf_a if gslot < 2 else mk.Sbf_b
                                go = 2048 * (gslot % 2)
                                sync.dma_start(
                                    out=gt[:, go + 512 * q4: go + 512 * (q4 + 1)],
                                    in_=mk.gnt_scr[t_i, q, :, :]).then_inc(mk.s_ld.h, 16)
                        mk.s_ld.n += 64

                        @blk.tensor
                        def _(t, cp=cp, half=half, ti4=ti4, gslot=gslot, units=units,
                              w=(bank_cons[ti4] if half == 0 else None), ld=mk.s_ld.n):
                            t.wait_ge(mk.s_ld.h, ld)
                            wait_evt(t, units[half][1])
                            wait_evt(t, w)
                            last = None
                            ub = units[half][0]
                            for kc in range(16):
                                q4, i = kc // 4, kc % 4
                                gt = mk.Sbf_a if gslot < 2 else mk.Sbf_b
                                go = 2048 * (gslot % 2)
                                last = t.matmul(banks4[ti4][:, :],
                                                gt[:, go + 512 * q4 + 128 * i: go + 512 * q4 + 128 * (i + 1)],
                                                mk.Wbuf[:, ub + kc * 512: ub + (kc + 1) * 512],
                                                start=(half == 0 and kc == 0),
                                                stop=(half == 1 and kc == 15))
                            last.then_inc(mk.s_pe.h, 1)
                        mk.s_pe.n += 1
                        gt_cons[gslot] = (mk.s_pe, mk.s_pe.n)
                    self.unit_done[uidx - 2 + half] = (mk.s_pe, mk.s_pe.n)
                mm_done = mk.s_pe.n

                for ti4 in range(4):
                    t_i = 4 * tq + ti4
                    lo = 3 + 128 * t_i
                    oslot = ti4 % 2

                    @blk.gpsimd
                    def _(g, cp=cp, t_i=t_i, lo=lo, oslot=oslot,
                          w1=op_cons[oslot], w2=self.ffn_stored):
                        wait_evts(g, w1, w2)
                        g.dma_start(out=mk.S32b[:, 512 * oslot: 512 * (oslot + 1)],
                                    in_=mk.x_in[lo: lo + 128, 512 * cp: 512 * (cp + 1)]
                                    ).then_inc(mk.s_w.h, 16)
                        g.dma_start(out=mk.S32b[:, 1024 + 512 * oslot: 1536 + 512 * oslot],
                                    in_=mk.ffn_scr[128 * t_i: 128 * (t_i + 1), 512 * cp: 512 * (cp + 1)]
                                    ).then_inc(mk.s_w.h, 16)
                    mk.s_w.n += 32

                    @blk.vector
                    def _(v, ti4=ti4, oslot=oslot, mm_done=mm_done, w=out_cons[oslot], ld=mk.s_w.n):
                        v.wait_ge(mk.s_w.h, ld)
                        v.wait_ge(mk.s_pe.h, mm_done)
                        wait_evt(v, w)
                        osl = mk.S32b[:, 2048 + 512 * oslot: 2560 + 512 * oslot]
                        v.tensor_tensor(out=osl, in0=banks4[ti4][:, :],
                                        in1=mk.S32b[:, 512 * oslot: 512 * (oslot + 1)], op=OP.add)
                        v.drain()
                        v.tensor_tensor(out=osl, in0=osl,
                                        in1=mk.S32b[:, 1024 + 512 * oslot: 1536 + 512 * oslot],
                                        op=OP.add).then_inc(mk.s_dve.h, 1)
                    mk.s_dve.n += 1
                    op_cons[oslot] = (mk.s_dve, mk.s_dve.n)
                    bank_cons[ti4] = (mk.s_dve, mk.s_dve.n)

                    @blk.sync
                    def _(sync, cp=cp, t_i=t_i, oslot=oslot):
                        sync.wait_ge(mk.s_dve.h, mk.s_dve.n)
                        sync.dma_start(out=mk.out[128 * t_i: 128 * (t_i + 1), 512 * cp: 512 * (cp + 1)],
                                       in_=mk.S32b[:, 2048 + 512 * oslot: 2560 + 512 * oslot]
                                       ).then_inc(mk.s_fin.h, 16)
                    mk.s_fin.n += 16
                    out_cons[oslot] = (mk.s_fin, mk.s_fin.n)

    def finalize_program(self):
        mk = self
        blk = self.block
        outs = []
        for name, sb in self.dbg_items:
            apx = sb.ap() if callable(getattr(sb, "ap", None)) else sb
            shp = list(apx.tensor.shape)
            dt = apx.tensor.dtype
            o = self.nc.dram_tensor("dbg_" + name, shp, dt, kind="ExternalOutput").ap()
            outs.append((o, apx))

        @blk.sync
        def _(sync):
            sync.wait_ge(mk.s_dve.h, mk.s_dve.n)
            sync.wait_ge(mk.s_act.h, mk.s_act.n)
            sync.wait_ge(mk.s_pe.h, mk.s_pe.n)
            sync.wait_ge(mk.s_pool.h, mk.s_pool.n)
            sync.wait_ge(mk.s_st.h, mk.s_st.n)
            sync.wait_ge(mk.s_w.h, mk.s_w.n)
            if mk.s_rows.n:
                sync.wait_ge(mk.s_rows.h, mk.s_rows.n)
            if mk.s_gst.n:
                sync.wait_ge(mk.s_gst.h, mk.s_gst.n)
            if mk.s_cc.n:
                sync.wait_ge(mk.s_cc.h, mk.s_cc.n)
            for o, sb in outs:
                sync.dma_start(out=o, in_=sb).then_inc(mk.s_fin.h, 16)
                mk.s_fin.n += 16
            if mk.s_fin.n == 0:
                sync.dma_start(out=mk.out[0:128, :], in_=mk.S32a[:, :]).then_inc(mk.s_fin.h, 16)
                mk.s_fin.n += 16
            sync.wait_ge(mk.s_fin.h, mk.s_fin.n)


def build_nc(stage=STAGE, debug=DEBUG):
    m = MK(stage=stage, debug=debug)
    nc = m.build()
    return nc, m


def shard_inputs(inputs):
    x = np.asarray(inputs["L0_attn_hidden_states"], np.float32)
    full = {
        "Wp": np.ascontiguousarray(np.asarray(inputs["L0_attn_in_proj_w"], np.float32)),
        "conv_w": np.ascontiguousarray(np.asarray(inputs["L0_attn_conv_w"], np.float32)),
        "A_log": np.ascontiguousarray(np.asarray(inputs["L0_attn_A_log"], np.float32)),
        "dt_b": np.ascontiguousarray(np.asarray(inputs["L0_attn_dt_bias"], np.float32)),
        "D_in": np.ascontiguousarray(np.asarray(inputs["L0_attn_D"], np.float32)),
        "ln_w": np.ascontiguousarray(np.asarray(inputs["L0_attn_ln_w"], np.float32)),
        "norm_w": np.ascontiguousarray(np.asarray(inputs["L0_attn_norm_w"], np.float32)),
        "Wout": np.ascontiguousarray(np.asarray(inputs["L0_attn_out_proj_w"], np.float32)),
        "mup": np.ascontiguousarray(np.asarray(inputs["L0_attn_mup_vector"], np.float32)),
        "Wg": np.ascontiguousarray(np.asarray(inputs["L0_ffn_W_gate"], np.float32)),
        "bg": np.ascontiguousarray(np.asarray(inputs["L0_ffn_b_gate"], np.float32)),
        "W_e": np.ascontiguousarray(np.asarray(inputs["L0_ffn_W_e"], np.float32)),
        "b_e": np.ascontiguousarray(np.asarray(inputs["L0_ffn_b_e"], np.float32)),
    }
    in_maps = []
    for c in range(8):
        b, q = c // 4, c % 4
        lo = 1024 * q - 3
        if lo < 0:
            xc = np.concatenate([np.zeros((3, 4096), np.float32), x[b, 0:1024]], 0)
        else:
            xc = np.ascontiguousarray(x[b, lo:1024 * q + 1024])
        selv = np.zeros((64, 4), np.float32)
        mulv = np.zeros((64, 4), np.float32)
        selv[:, :q] = 1.0
        mulv[:, :q] = 1.0
        m = dict(full)
        m["x_in"] = xc
        m["selv"] = selv
        m["mulv"] = mulv
        in_maps.append(m)
    return in_maps


def kernel(**inputs):
    in_maps = shard_inputs(inputs)
    nc, _ = build_nc(stage=7, debug=False)
    res = run_bass_kernel_spmd(nc, in_maps, core_ids=list(range(8)))
    outs = [res.results[c]["out"] for c in range(8)]
    return np.stack([np.concatenate(outs[4 * b: 4 * b + 4], 0) for b in range(2)], 0)
